# version 08
# baseline (speedup 1.0000x reference)
import numpy as np
import ml_dtypes
import concourse.bass as bass
import concourse.mybir as mybir
from concourse.bass_utils import run_bass_kernel_spmd

F32 = mybir.dt.float32
BF16 = mybir.dt.bfloat16
ALU = mybir.AluOpType
AF = mybir.ActivationFunctionType
AX = mybir.AxisListType.X

C = 1024
NG = C // 128
WIN = 512
NQT = 4
DH = 64
H = 8
INNER = 512
NIC = INNER // 128
HID = 2730
SX = 455
SXCH = [128, 128, 128, 71]
KCLAMP = float(np.exp(np.float64(60.0)))
NEGM = -1.0e30
EPSWK = 1e-16

ENGS = ("tensor", "vector", "scalar", "sync", "gpsimd")


class Sched:

    def __init__(self):
        self.q = {e: [] for e in ENGS}
        self.c = dict(tensor=0, vector=0, scalar=0, sync=0, gpsimd=0,
                      d_sy=0, d_ge=0, d_out=0, d_x0=0, d_x1=0)
        self.sealed = {}
        self.sem = {}

    def op(self, e, fn):
        self.q[e].append(fn)

    def step(self, e, fn):
        self.c[e] += 1

        def run(eng, fn=fn, k=e):
            fn(eng).then_inc(self.sem[k], 1)
        self.q[e].append(run)
        return self.c[e]

    def wait(self, e, key, val=None):
        v = self.c[key] if val is None else val
        if v <= 0:
            return
        if key in self.q and e != key and v == self.c[key] \
                and self.sealed.get(key) != v:
            self.sealed[key] = v

            def seal(eng, key=key, v=v):
                eng.wait_ge(self.sem[key], v)
            self.q[key].append(seal)

        def run(eng, key=key, v=v):
            eng.wait_ge(self.sem[key], v)
        self.q[e].append(run)

    def dma(self, e, out, in_, slow=False, key=None):
        if key is None:
            key = {"sync": "d_sy", "gpsimd": "d_ge"}[e]
        self.c[key] += 1

        def run(eng, out=out, in_=in_, key=key, slow=slow):
            eng.dma_start(out=out, in_=in_,
                          allow_slow_non_contiguous=slow).then_inc(self.sem[key], 16)
        self.q[e].append(run)
        return self.c[key]

    def dma_out(self, out, in_):
        self.c["d_out"] += 1

        def run(eng, out=out, in_=in_):
            eng.dma_start(out=out, in_=in_, accum_op=ALU.add).then_inc(self.sem["d_out"], 16)
        self.q["gpsimd"].append(run)
        return self.c["d_out"]

    def wait_dma(self, e, key, val=None):
        v = self.c[key] if val is None else val
        if v <= 0:
            return

        def run(eng, key=key, v=v):
            eng.wait_ge(self.sem[key], 16 * v)
        self.q[e].append(run)

    def drain(self, e, key=None):
        if key is None:
            key = {"sync": "d_sy", "gpsimd": "d_ge"}[e]
        self.wait_dma(e, key)

    def seq(self, e, fns):
        for i, fn in enumerate(fns):
            if i > 0:
                self.wait(e, e, self.c[e])
            self.step(e, fn)
        return self.c[e]

    def bump(self, e):
        self.c[e] += 1

        def run(eng, k=e):
            eng.nop().then_inc(self.sem[k], 1)
        self.q[e].append(run)
        return self.c[e]

    def barrier(self):
        self.wait_dma("sync", "d_sy")
        self.wait_dma("sync", "d_x0")
        self.wait_dma("sync", "d_x1")
        self.wait_dma("gpsimd", "d_ge")
        self.wait_dma("gpsimd", "d_out")
        tgt = {e: self.bump(e) for e in ENGS}
        for e in ENGS:
            for o in ENGS:
                if o != e:
                    self.wait(e, o, tgt[o])


def build_nc(PRE, OWN):
    NT = (PRE + OWN) // WIN
    W0 = PRE // WIN
    TB = PRE + OWN
    NCH = NT // 2
    assert NT % 2 == 0 and W0 >= 5 and W0 % 2 == 0 and NT <= 16

    nc = bass.Bass("TRN2", target_bir_lowering=False)
    dt_ = nc.dram_tensor

    xT = dt_("xT", [C, TB], F32, kind="ExternalInput").ap()
    outT = dt_("outT", [C, OWN], F32, kind="ExternalOutput").ap()
    wkin = dt_("wkin", [C, C], F32, kind="ExternalInput").ap()
    wvin = dt_("wvin", [C, C], F32, kind="ExternalInput").ap()
    wrin = dt_("wrin", [C, C], F32, kind="ExternalInput").ap()
    woin = dt_("woin", [C, C], F32, kind="ExternalInput").ap()
    tparams = dt_("tparams", [128, 3 * NG], F32, kind="ExternalInput").ap()
    wqkv = dt_("wqkv", [2, C, 3 * INNER], F32, kind="ExternalInput").ap()
    wof = dt_("wof", [2, INNER, C], F32, kind="ExternalInput").ap()
    w1 = dt_("w1", [2, C, 2 * HID], F32, kind="ExternalInput").ap()
    w2 = dt_("w2", [2, HID, C], F32, kind="ExternalInput").ap()
    svecs = dt_("svecs", [128, 2 * (3 * NG + 2)], F32, kind="ExternalInput").ap()
    b1p = dt_("b1p", [128, 2 * 6 * 8], F32, kind="ExternalInput").ap()
    ident_in = dt_("ident_in", [128, 128], BF16, kind="ExternalInput").ap()
    perm_in = dt_("perm_in", [128, 128], BF16, kind="ExternalInput").ap()
    bc16in = dt_("bc16in", [16, 16 * 128], F32, kind="ExternalInput").ap()
    hindin = dt_("hindin", [128, NIC * H], F32, kind="ExternalInput").ap()
    hindTin = dt_("hindTin", [H, INNER], F32, kind="ExternalInput").ap()
    masks_in = dt_("masks_in", [128, 260], F32, kind="ExternalInput").ap()
    mask_w0 = dt_("mask_w0", [WIN, 2 * WIN], F32, kind="ExternalInput").ap()
    rott_in = dt_("rott_in", [128, 4 * TB], BF16, kind="ExternalInput").ap()

    s_wqkv = dt_("s_wqkv", [2, C, 3 * INNER], BF16, kind="Internal").ap()
    s_wof = dt_("s_wof", [2, INNER, C], BF16, kind="Internal").ap()
    s_w1 = dt_("s_w1", [2, C, 2 * HID], BF16, kind="Internal").ap()
    s_w2 = dt_("s_w2", [2, HID, C], BF16, kind="Internal").ap()

    from contextlib import ExitStack
    es = ExitStack()
    sb = lambda n, s, d: es.enter_context(nc.sbuf_tensor(n, s, d))
    pst = lambda n, s, d: es.enter_context(nc.psum_tensor(n, s, d))

    wuni = sb("wuni", [128, 24576], BF16)
    S1 = 0
    S2 = 16384
    uA = sb("uA", [128, 8192], F32)
    uC = sb("uC", [128, 22784], BF16)
    xn_c = sb("xn_c", [128, NG * (2 * WIN + 1)], BF16)
    agxw = sb("agxw", [128, 12288], BF16)
    kp = sb("kp", [128, 2 * NIC * WIN], BF16)
    vp = sb("vp", [128, 2 * NQT * INNER], BF16)
    stats = sb("stats", [16, 2 * WIN], F32)
    strow = sb("strow", [1, 2048], F32)
    bc16 = sb("bc16", [16, 16 * 128], F32)
    hind = sb("hind", [128, NIC * H], F32)
    hindT = sb("hindT", [H, INNER], F32)
    cons_f = sb("cons_f", [128, 260], F32)
    ident = sb("ident", [128, 128], BF16)
    permt = sb("permt", [128, 128], BF16)
    tp = sb("tp", [128, 3 * NG], F32)
    dec = sb("dec", [128, 2 * NG], F32)
    sv = sb("sv", [128, 2 * (3 * NG + 2)], F32)
    b1s = sb("b1s", [128, 2 * 6 * 8], F32)
    scst = sb("scst", [128, 2 * NG], F32)
    xprevc = sb("xprevc", [128, NG], F32)
    colc = sb("colc", [128, 5 * NG], F32)
    essf = sb("essf", [128, 8], F32)
    rn8 = sb("rn8", [H, WIN], F32)

    R_xmt, R_rwkv, R_sig = 0, 4096, 8192
    R_kx, R_vv, R_skv = 8704, 9728, 10752
    R_wkv, R_rcp, R_sqs, R_aot = 12804, 14852, 16900, 18948
    F_qk, F_vtm, F_lno, F_o1 = 0, 8192, 12288, 16384
    F_eb, F_pt, F_t1, F_sq2 = 18432, 19072, 19712, 21760
    F_mw = F_t1
    assert F_sq2 + 1024 <= 22784 and R_aot + 1024 <= 22784

    PA = [pst(f"PA{i}", [128, 512], F32) for i in range(4)]
    PB = pst("PB", [128, 1024], F32)
    PS = pst("PS", [128, 512], F32)
    PT = pst("PT", [128, 512], F32)

    S = Sched()
    ones = cons_f[:, 256:257]
    smask = cons_f[:, 257:258]

    S.dma("sync", cons_f[:, :], masks_in)
    S.dma("sync", ident[:, :], ident_in)
    S.dma("sync", permt[:, :], perm_in)
    S.dma("sync", bc16[:, :], bc16in)
    S.dma("sync", hind[:, :], hindin)
    S.dma("sync", hindT[:, :], hindTin)
    S.dma("sync", tp[:, :], tparams)
    S.dma("sync", sv[:, :], svecs)
    S.dma("sync", b1s[:, :], b1p)
    for g in range(NG):
        S.dma("gpsimd", wuni[:, g * C:(g + 1) * C], wkin[g * 128:(g + 1) * 128, :])
        S.dma("gpsimd", wuni[:, 8192 + g * C:8192 + (g + 1) * C], wvin[g * 128:(g + 1) * 128, :])
    for j0 in range(0, OWN, 2048):
        j1 = min(j0 + 2048, OWN)
        S.dma_out(outT[:, j0:j1], xT[:, PRE + j0:PRE + j1])

    pieces = []
    for l in range(2):
        for (src, dst, rows) in ((wqkv[l], s_wqkv[l], C), (wof[l], s_wof[l], INNER),
                                 (w1[l], s_w1[l], C), (w2[l], s_w2[l], HID)):
            for r0 in range(0, rows, 128):
                r1 = min(r0 + 128, rows)
                pieces.append((src[r0:r1, :], dst[r0:r1, :], r1 - r0, src.shape[1]))
    for i, (src, dst, rr, cc) in enumerate(pieces):
        slot = i % 2
        ssl = uC[0:rr, slot * 8192: slot * 8192 + cc]
        if i >= 1:
            S.wait_dma("gpsimd", "d_sy")
        S.dma("gpsimd", ssl, src)
        S.wait_dma("sync", "d_ge")
        S.dma("sync", dst, ssl)

    S.wait_dma("scalar", "d_sy")

    k1 = S.step("scalar", lambda se: se.activation(out=dec[:, 0:NG], in_=tp[:, 0:NG],
                                                   func=AF.Exp))
    S.wait("scalar", "scalar", k1)
    S.step("scalar", lambda se: se.activation(out=dec[:, 0:NG], in_=dec[:, 0:NG],
                                              func=AF.Exp, scale=-1.0))
    S.step("scalar", lambda se: se.activation(out=dec[:, NG:2 * NG], in_=tp[:, NG:2 * NG],
                                              func=AF.Exp))

    def _z(ve):
        ve.memset(scst[:, :], 0.0)
        ve.memset(xprevc[:, :], 0.0)
        ve.memset(stats[:, :], 0.0)
        ve.memset(strow[:, :], 0.0)
        return ve.memset(colc[:, :], 0.0)
    S.step("vector", _z)
    S.barrier()

    f32v = lambda off, n: uC[:, off:off + 2 * n].bitcast(F32)
    kxb = f32v(R_kx, 512)
    vvb = f32v(R_vv, 512)
    skv = f32v(R_skv, 1026)
    wkvb = f32v(R_wkv, 1024)
    rcpb = f32v(R_rcp, 1024)
    sqs2 = f32v(R_sqs, 1024)
    aot = f32v(R_aot, 512)
    xmt = uC[:, R_xmt:R_xmt + NG * WIN]
    rwkvb = uC[:, R_rwkv:R_rwkv + NG * WIN]
    sigb = uC[:, R_sig:R_sig + WIN]

    last_xuse = [0, 0]
    for t in range(NT):
        own = t >= W0
        buf = t % 2
        xts = uA[:, buf * 4096:buf * 4096 + NG * WIN]
        S.wait("sync", "vector", last_xuse[buf])
        xkey = f"d_x{buf}"
        for g in range(NG):
            S.dma("sync", xts[:, g * WIN:(g + 1) * WIN],
                  xT[g * 128:(g + 1) * 128, t * WIN:(t + 1) * WIN], key=xkey)
        S.drain("sync", xkey)
        S.wait_dma("scalar", xkey)
        S.wait_dma("tensor", xkey)
        S.wait("tensor", "vector")
        for g in range(NG):
            sq = sqs2[:, (g % 2) * 512:(g % 2) * 512 + 512]
            S.wait("scalar", "tensor")
            sc = S.step("scalar", lambda se, g=g, sq=sq, xts=xts:
                        se.activation(out=sq, in_=xts[:, g * WIN:(g + 1) * WIN], func=AF.Square))
            S.wait("tensor", "scalar", sc)

            def _mm(te, g=g, sq=sq, xts=xts):
                te.matmul(PS[0:1, 0:WIN], ones, xts[:, g * WIN:(g + 1) * WIN],
                          start=(g == 0), stop=(g == NG - 1))
                return te.matmul(PB[0:1, 0:WIN], ones, sq,
                                 start=(g == 0), stop=(g == NG - 1))
            S.step("tensor", _mm)

        S.wait("vector", "tensor")
        S.seq("vector", [
            lambda ve: ve.tensor_scalar(out=strow[0:1, 1024:1024 + WIN], in0=PS[0:1, 0:WIN],
                                        scalar1=1.0 / C, scalar2=None, op0=ALU.mult),
            lambda ve: ve.tensor_mul(out=strow[0:1, 0:WIN],
                                     in0=strow[0:1, 1024:1024 + WIN],
                                     in1=strow[0:1, 1024:1024 + WIN]),
            lambda ve: ve.scalar_tensor_tensor(out=strow[0:1, 0:WIN],
                                               in0=PB[0:1, 0:WIN], scalar=1.0 / C,
                                               in1=strow[0:1, 0:WIN],
                                               op0=ALU.mult, op1=ALU.subtract),
        ])
        S.wait("scalar", "vector")
        S.seq("scalar", [
            lambda se: se.activation(out=strow[0:1, 0:WIN], in_=strow[0:1, 0:WIN],
                                     func=AF.Ln, bias=cons_f[0:1, 258:259]),
            lambda se: se.activation(out=strow[0:1, 0:WIN], in_=strow[0:1, 0:WIN],
                                     func=AF.Exp, scale=-0.5),
        ])
        S.wait("vector", "scalar")
        S.step("vector", lambda ve: ve.tensor_mul(
            out=strow[0:1, WIN:2 * WIN], in0=strow[0:1, 0:WIN],
            in1=strow[0:1, 1024:1024 + WIN]))
        S.wait("sync", "vector")
        S.dma("sync", stats[t:t + 1, 0:2 * WIN], strow[0:1, 0:2 * WIN])
        S.drain("sync")
        S.wait_dma("tensor", "d_sy")
        S.wait("tensor", "vector")

        def _bc(te, t=t):
            te.matmul(PA[2][:, 0:WIN], bc16[:, t * 128:(t + 1) * 128],
                      stats[:, 0:WIN], start=True, stop=True)
            return te.matmul(PA[3][:, 0:WIN], bc16[:, t * 128:(t + 1) * 128],
                             stats[:, WIN:2 * WIN], start=True, stop=True)
        S.step("tensor", _bc)
        S.wait("vector", "tensor")
        fns = []
        for g in range(NG):
            xg = xts[:, g * WIN:(g + 1) * WIN]
            xn = sqs2[:, 0:WIN]
            tmv = tp[:, 2 * NG + g:2 * NG + g + 1]
            xmg = xmt[:, g * WIN:(g + 1) * WIN]
            fns += [
                lambda ve, xg=xg, xn=xn: ve.tensor_mul(out=xn, in0=xg, in1=PA[2][:, 0:WIN]),
                lambda ve, xn=xn: ve.tensor_sub(out=xn, in0=xn, in1=PA[3][:, 0:WIN]),
                lambda ve, xn=xn: ve.tensor_sub(out=sqs2[:, 512:512 + WIN - 1],
                                                in0=xn[:, 1:WIN], in1=xn[:, 0:WIN - 1]),
                lambda ve, xn=xn, g=g: ve.tensor_sub(out=sqs2[:, 512 + WIN - 1:512 + WIN],
                                                     in0=xn[:, 0:1], in1=xprevc[:, g:g + 1]),
                lambda ve, xmg=xmg, tmv=tmv, xn=xn: ve.scalar_tensor_tensor(
                    out=xmg[:, 1:WIN], in0=sqs2[:, 512:512 + WIN - 1], scalar=tmv,
                    in1=xn[:, 0:WIN - 1], op0=ALU.mult, op1=ALU.add),
                lambda ve, xmg=xmg, tmv=tmv, g=g: ve.scalar_tensor_tensor(
                    out=xmg[:, 0:1], in0=sqs2[:, 512 + WIN - 1:512 + WIN], scalar=tmv,
                    in1=xprevc[:, g:g + 1], op0=ALU.mult, op1=ALU.add),
                lambda ve, xn=xn, g=g: ve.tensor_copy(out=xprevc[:, g:g + 1],
                                                      in_=xn[:, WIN - 1:WIN]),
            ]
        S.seq("vector", fns)
        last_xuse[buf] = S.c["vector"]

        if own:
            S.wait("gpsimd", "tensor")
            for g in range(NG):
                S.dma("gpsimd", wuni[:, 16384 + g * C:16384 + (g + 1) * C],
                      wrin[g * 128:(g + 1) * 128, :])
            wr_ld = S.c["d_ge"]
            S.drain("gpsimd")

        for g in range(NG):
            S.wait("tensor", "vector")
            S.wait("tensor", "scalar")

            def _kvm(te, g=g):
                for d in range(NG):
                    te.matmul(PA[0][:, 0:WIN], wuni[:, d * C + g * 128:d * C + (g + 1) * 128],
                              xmt[:, d * WIN:(d + 1) * WIN], start=(d == 0), stop=(d == NG - 1))
                for d in range(NG):
                    r = te.matmul(PA[1][:, 0:WIN],
                                  wuni[:, 8192 + d * C + g * 128:8192 + d * C + (g + 1) * 128],
                                  xmt[:, d * WIN:(d + 1) * WIN], start=(d == 0), stop=(d == NG - 1))
                return r
            S.step("tensor", _kvm)
            S.wait("scalar", "tensor")
            S.wait("scalar", "vector")

            def _ev(se):
                se.activation(out=kxb[:, :], in_=PA[0][:, 0:WIN], func=AF.Exp)
                return se.copy(out=vvb[:, :], in_=PA[1][:, 0:WIN])
            S.step("scalar", _ev)
            S.wait("vector", "scalar")
            fw = dec[:, NG + g:NG + g + 1]
            fns = [
                lambda ve: ve.tensor_scalar_min(out=kxb[:, :], in0=kxb[:, :], scalar1=KCLAMP),
                lambda ve: ve.tensor_mul(out=vvb[:, :], in0=kxb[:, :], in1=vvb[:, :]),
            ]
            if t == W0:
                fns += [
                    lambda ve, g=g: ve.tensor_mul(out=scst[:, g:g + 1],
                                                  in0=scst[:, g:g + 1], in1=smask),
                    lambda ve, g=g: ve.tensor_mul(out=scst[:, NG + g:NG + g + 1],
                                                  in0=scst[:, NG + g:NG + g + 1], in1=smask),
                ]
            fns += [
                lambda ve, g=g: ve.tensor_copy(out=skv[:, 0:1], in_=scst[:, g:g + 1]),
                lambda ve, g=g: ve.tensor_tensor_scan(
                    out=skv[:, 1:WIN + 1], data0=dec[:, g:g + 1].broadcast_to([128, WIN]),
                    data1=vvb[:, :], initial=scst[:, g:g + 1], op0=ALU.mult, op1=ALU.add),
                lambda ve, g=g: ve.tensor_copy(out=scst[:, g:g + 1], in_=skv[:, WIN:WIN + 1]),
                lambda ve, g=g: ve.tensor_copy(out=skv[:, WIN + 1:WIN + 2],
                                               in_=scst[:, NG + g:NG + g + 1]),
                lambda ve, g=g: ve.tensor_tensor_scan(
                    out=skv[:, WIN + 2:2 * WIN + 2],
                    data0=dec[:, g:g + 1].broadcast_to([128, WIN]),
                    data1=kxb[:, :], initial=scst[:, NG + g:NG + g + 1],
                    op0=ALU.mult, op1=ALU.add),
                lambda ve, g=g: ve.tensor_copy(out=scst[:, NG + g:NG + g + 1],
                                               in_=skv[:, 2 * WIN + 1:2 * WIN + 2]),
            ]
            if own:
                fns += [
                    lambda ve, fw=fw: ve.scalar_tensor_tensor(
                        out=wkvb[:, 0:WIN], in0=vvb[:, :], scalar=fw,
                        in1=skv[:, 0:WIN], op0=ALU.mult, op1=ALU.add),
                    lambda ve, fw=fw: ve.scalar_tensor_tensor(
                        out=wkvb[:, WIN:2 * WIN], in0=kxb[:, :], scalar=fw,
                        in1=skv[:, WIN + 1:2 * WIN + 1], op0=ALU.mult, op1=ALU.add),
                    lambda ve: ve.tensor_scalar_add(out=wkvb[:, WIN:2 * WIN],
                                                    in0=wkvb[:, WIN:2 * WIN], scalar1=EPSWK),
                ]
            S.seq("vector", fns)

            if own:
                if g == 0:
                    S.wait_dma("tensor", "d_ge", wr_ld)
                S.wait("tensor", "scalar")

                def _r(te, g=g):
                    for d in range(NG):
                        r = te.matmul(PA[0][:, 0:WIN],
                                      wuni[:, 16384 + d * C + g * 128:16384 + d * C + (g + 1) * 128],
                                      xmt[:, d * WIN:(d + 1) * WIN], start=(d == 0),
                                      stop=(d == NG - 1))
                    return r
                S.step("tensor", _r)
                S.wait("scalar", "tensor")
                S.wait("scalar", "vector")
                S.step("scalar", lambda se: se.activation(out=sqs2[:, 0:WIN],
                                                          in_=PA[0][:, 0:WIN],
                                                          func=AF.Exp, scale=-1.0))
                S.wait("vector", "scalar")
                S.step("vector", lambda ve: ve.tensor_scalar_add(
                    out=sqs2[:, 512:1024], in0=sqs2[:, 0:WIN], scalar1=1.0))
                S.wait("scalar", "vector")
                S.seq("scalar", [
                    lambda se: se.activation(out=rcpb[:, WIN:2 * WIN],
                                             in_=sqs2[:, 512:1024], func=AF.Ln),
                    lambda se: se.activation(out=sigb[:, :], in_=rcpb[:, WIN:2 * WIN],
                                             func=AF.Exp, scale=-1.0),
                    lambda se: se.activation(out=rcpb[:, 0:WIN],
                                             in_=wkvb[:, WIN:2 * WIN], func=AF.Ln),
                    lambda se: se.activation(out=rcpb[:, 0:WIN], in_=rcpb[:, 0:WIN],
                                             func=AF.Exp, scale=-1.0),
                ])
                S.wait("vector", "scalar")
                S.seq("vector", [
                    lambda ve: ve.tensor_mul(out=wkvb[:, 0:WIN], in0=wkvb[:, 0:WIN],
                                             in1=rcpb[:, 0:WIN]),
                    lambda ve, g=g: ve.tensor_mul(out=rwkvb[:, g * WIN:(g + 1) * WIN],
                                                  in0=wkvb[:, 0:WIN], in1=sigb[:, :]),
                ])

        if own:
            S.wait("gpsimd", "tensor")
            for g in range(NG):
                S.dma("gpsimd", wuni[:, 16384 + g * C:16384 + (g + 1) * C],
                      woin[g * 128:(g + 1) * 128, :])
            wo_ld = S.c["d_ge"]
            S.drain("gpsimd")
            S.wait_dma("tensor", "d_ge", wo_ld)
            S.wait("tensor", "vector")
            for oc in range(NG):
                pb = PA[oc % 2]
                S.wait("tensor", "scalar")

                def _wo(te, oc=oc, pb=pb):
                    for g in range(NG):
                        r = te.matmul(pb[:, 0:WIN],
                                      wuni[:, 16384 + g * C + oc * 128:16384 + g * C + (oc + 1) * 128],
                                      rwkvb[:, g * WIN:(g + 1) * WIN], start=(g == 0),
                                      stop=(g == NG - 1))
                    return r
                S.step("tensor", _wo)
                S.wait("scalar", "tensor")
                S.wait_dma("scalar", "d_out")
                S.step("scalar", lambda se, pb=pb: se.copy(out=aot[:, :], in_=pb[:, 0:WIN]))
                S.wait("gpsimd", "scalar")
                S.dma_out(outT[oc * 128:(oc + 1) * 128, (t - W0) * WIN:(t - W0 + 1) * WIN],
                          aot[:, :])

    S.barrier()

    FA = {0: W0 - 5, 1: W0 - 2}
    FB = {0: W0 - 4, 1: W0 - 1}
    FC = {0: W0 - 3, 1: W0}
    tt = FA[0] - 1
    xcol = f32v(R_kx, NG + NG)
    for g in range(NG):
        S.dma("sync", xcol[:, g:g + 1],
              xT[g * 128:(g + 1) * 128, (tt + 1) * WIN - 1:(tt + 1) * WIN], slow=True)
    S.drain("sync")
    S.wait_dma("tensor", "d_sy")

    def _bcc(te, tt=tt):
        te.matmul(PS[:, 0:1], bc16[:, tt * 128:(tt + 1) * 128],
                  stats[:, WIN - 1:WIN], start=True, stop=True)
        return te.matmul(PS[:, 1:2], bc16[:, tt * 128:(tt + 1) * 128],
                         stats[:, 2 * WIN - 1:2 * WIN], start=True, stop=True)
    S.step("tensor", _bcc)
    S.wait("vector", "tensor")

    fns = []
    for g in range(NG):
        fns += [
            lambda ve, g=g: ve.tensor_mul(out=xcol[:, NG + g:NG + g + 1],
                                          in0=xcol[:, g:g + 1], in1=PS[:, 0:1]),
            lambda ve, g=g: ve.tensor_sub(out=colc[:, 3 * NG + g:3 * NG + g + 1],
                                          in0=xcol[:, NG + g:NG + g + 1], in1=PS[:, 1:2]),
        ]
    S.seq("vector", fns)
    S.barrier()

    CW = 2 * WIN

    def str_sl(g, c0, c1):
        return uA[:, g * CW + c0:g * CW + c1]

    qkb = uC[:, F_qk:F_qk + 2 * NIC * CW]
    rott = uC[:, F_lno:F_lno + 4 * CW]
    vtm = uC[:, F_vtm:F_vtm + 8 * 512]
    lno = uC[:, F_lno:F_lno + NG * WIN]
    o1b = uC[:, F_o1:F_o1 + NIC * WIN]
    ebuf = uC[:, F_eb:F_eb + 640]
    ptb = uC[:, F_pt:F_pt + 640]
    t1b = uC[:, F_t1:F_t1 + 2048].bitcast(F32)
    sq2f = uC[:, F_sq2:F_sq2 + 1024].bitcast(F32)
    mw0s = uC[:, F_mw:F_mw + 1280].bitcast(F32)
    agb = agxw[:, 0:24 * WIN]
    xwv = agxw[:, 0:NG * 1028].bitcast(F32)

    def q_sl(kind, ic, c0, c1):
        base = 0 if kind == "q" else NIC * CW
        return qkb[:, base + ic * CW + c0:base + ic * CW + c1]

    def inp_sl(g, c0, c1, shifted):
        off = 0 if shifted else 1
        return xn_c[:, g * (CW + 1) + off + c0:g * (CW + 1) + off + c1]

    w2psum = [PA[0][:, 0:512], PA[1][:, 0:512], PA[2][:, 0:512], PA[3][:, 0:512],
              PB[:, 0:512], PB[:, 512:1024], PS[:, 0:512], PT[:, 0:512]]

    for ch in range(NCH):
        wlo, whi = 2 * ch, 2 * ch + 2
        if whi <= FA[0]:
            continue
        if whi > FA[1]:
            S.wait("vector", "vector", S.c["vector"])

            def _hcp(ve):
                for g in range(NG):
                    r = ve.tensor_copy(out=colc[:, 1 * NG + g:1 * NG + g + 1],
                                       in_=colc[:, 4 * NG + g:4 * NG + g + 1])
                return r
            S.step("vector", _hcp)
        if wlo == W0:
            S.wait("vector", "vector", S.c["vector"])
            S.step("vector", lambda ve: ve.tensor_scalar_mul(
                out=colc[:, 0:4 * NG], in0=colc[:, 0:4 * NG], scalar1=smask))
        for l in range(2):
            wsA = [w for w in range(wlo, whi) if w >= FA[l]]
            wsB = [w for w in range(wlo, whi) if w >= FB[l]]
            wsC = [w for w in range(wlo, whi) if w >= FC[l]]
            if not wsA:
                continue

            S.wait("sync", "tensor")
            for g in range(NG):
                S.dma("sync", wuni[:, S1 + g * 1536:S1 + (g + 1) * 1536],
                      s_wqkv[l, g * 128:(g + 1) * 128, :])
            for ic in range(NIC):
                S.dma("sync", wuni[:, S2 + ic * C:S2 + (ic + 1) * C],
                      s_wof[l, ic * 128:(ic + 1) * 128, :])
            S.wait("sync", "vector")
            for ti in range(4):
                S.dma("sync", rott[:, ti * CW:(ti + 1) * CW],
                      rott_in[:, ti * TB + wlo * WIN:ti * TB + whi * WIN])
            qkv_ld = S.c["d_sy"]
            S.drain("sync")

            S.wait("vector", "tensor")

            cw0 = wsA[0] - wlo
            S.wait("vector", "vector", S.c["vector"])

            def _hal(ve, l=l, cw0=cw0):
                src_base = (3 if l == 0 else 1) * NG
                for g in range(NG):
                    r = ve.tensor_copy(
                        out=xn_c[:, g * (CW + 1) + cw0 * WIN:g * (CW + 1) + cw0 * WIN + 1],
                        in_=colc[:, src_base + g:src_base + g + 1])
                return r
            S.step("vector", _hal)

            for w in wsA:
                cw = w - wlo
                if l == 0:
                    S.wait("sync", "tensor")
                    S.wait("sync", "vector")
                    for g in range(NG):
                        S.dma("sync", xwv[:, g * 514:g * 514 + WIN],
                              xT[g * 128:(g + 1) * 128, w * WIN:(w + 1) * WIN])
                    xld = S.c["d_sy"]
                    S.drain("sync")
                    S.wait_dma("tensor", "d_sy", xld)
                    S.wait("tensor", "vector")

                    def _b2(te, w=w):
                        te.matmul(PA[2][:, 0:WIN], bc16[:, w * 128:(w + 1) * 128],
                                  stats[:, 0:WIN], start=True, stop=True)
                        return te.matmul(PA[3][:, 0:WIN], bc16[:, w * 128:(w + 1) * 128],
                                         stats[:, WIN:2 * WIN], start=True, stop=True)
                    S.step("tensor", _b2)
                    S.wait("vector", "tensor")

                    fns = []
                    for g in range(NG):
                        dst = xn_c[:, g * (CW + 1) + 1 + cw * WIN:
                                   g * (CW + 1) + 1 + cw * WIN + WIN]
                        buf2 = t1b[:, (g % 2) * WIN:(g % 2) * WIN + WIN]
                        fns += [
                            lambda ve, g=g, buf2=buf2: ve.tensor_mul(
                                out=buf2, in0=xwv[:, g * 514:g * 514 + WIN],
                                in1=PA[2][:, 0:WIN]),
                            lambda ve, dst=dst, buf2=buf2: ve.tensor_sub(
                                out=dst, in0=buf2, in1=PA[3][:, 0:WIN]),
                        ]
                    S.seq("vector", fns)
                else:
                    S.wait("vector", "gpsimd")
                    S.wait("vector", "vector", S.c["vector"])

                    def _cst(ve, cw=cw):
                        for g in range(NG):
                            dst = xn_c[:, g * (CW + 1) + 1 + cw * WIN:
                                       g * (CW + 1) + 1 + cw * WIN + WIN]
                            r = ve.tensor_copy(out=dst, in_=str_sl(g, cw * WIN, cw * WIN + WIN))
                        return r
                    S.step("vector", _cst)

                S.wait_dma("tensor", "d_sy", qkv_ld)
                S.wait("tensor", "vector")
                for kind, cbase in (("q", 0), ("k", 512)):
                    for ic in range(NIC):
                        pb = PA[ic % 2]
                        S.wait("tensor", "scalar")

                        def _qk(te, cbase=cbase, ic=ic, pb=pb, cw=cw):
                            for d in range(NG):
                                r = te.matmul(pb[:, 0:WIN],
                                              wuni[:, S1 + d * 1536 + cbase + ic * 128:
                                                   S1 + d * 1536 + cbase + (ic + 1) * 128],
                                              inp_sl(d, cw * WIN, cw * WIN + WIN, d >= 4),
                                              start=(d == 0), stop=(d == NG - 1))
                            return r
                        S.step("tensor", _qk)
                        S.wait("scalar", "tensor")
                        S.step("scalar", lambda se, kind=kind, ic=ic, pb=pb, cw=cw:
                               se.copy(out=q_sl(kind, ic, cw * WIN, cw * WIN + WIN),
                                       in_=pb[:, 0:WIN]))
                for tc in range(NQT):
                    pb = PA[2 + (tc % 2)]
                    S.wait("tensor", "scalar")

                    def _v(te, tc=tc, pb=pb, cw=cw):
                        for d in range(NG):
                            r = te.matmul(pb[:, 0:WIN],
                                          inp_sl(d, cw * WIN + tc * 128,
                                                 cw * WIN + (tc + 1) * 128, d >= 4),
                                          wuni[:, S1 + d * 1536 + 1024:S1 + d * 1536 + 1536],
                                          start=(d == 0), stop=(d == NG - 1))
                        return r
                    S.step("tensor", _v)
                    S.wait("scalar", "tensor")
                    S.step("scalar", lambda se, tc=tc, pb=pb, cw=cw:
                           se.copy(out=vtm[:, (cw * NQT + tc) * 512:(cw * NQT + tc + 1) * 512],
                                   in_=pb[:, 0:WIN]))

                for kind in ("q", "k"):
                    scoff = (l * (3 * NG + 2)) + 3 * NG + (0 if kind == "q" else 1)
                    for ic in range(NIC):
                        S.wait("scalar", "tensor")
                        S.wait("scalar", "vector")
                        S.wait("scalar", "scalar", S.c["scalar"])
                        sc = S.step("scalar", lambda se, kind=kind, ic=ic, cw=cw:
                                    se.activation(out=sq2f[:, 0:WIN],
                                                  in_=q_sl(kind, ic, cw * WIN, cw * WIN + WIN),
                                                  func=AF.Square))
                        S.wait("tensor", "scalar", sc)
                        S.step("tensor", lambda te, ic=ic:
                               te.matmul(PS[0:H, 0:WIN], hind[:, ic * H:(ic + 1) * H],
                                         sq2f[:, 0:WIN], start=(ic == 0), stop=(ic == NIC - 1)))
                    S.wait("scalar", "tensor")
                    S.seq("scalar", [
                        lambda se: se.activation(out=rn8[0:H, 0:WIN], in_=PS[0:H, 0:WIN],
                                                 func=AF.Ln, bias=cons_f[0:H, 259:260]),
                        lambda se: se.activation(out=rn8[0:H, 0:WIN], in_=rn8[0:H, 0:WIN],
                                                 func=AF.Exp, scale=-0.5),
                    ])
                    S.wait("vector", "scalar")
                    S.step("vector", lambda ve: ve.tensor_scalar_min(
                        out=rn8[0:H, 0:WIN], in0=rn8[0:H, 0:WIN], scalar1=1e12))
                    for ic in range(NIC):
                        S.wait("tensor", "vector")
                        S.step("tensor", lambda te, ic=ic:
                               te.matmul(PS[:, 0:WIN], hindT[:, ic * 128:(ic + 1) * 128],
                                         rn8[0:H, 0:WIN], start=True, stop=True))
                        S.wait("vector", "tensor")

                        qs_ = q_sl(kind, ic, cw * WIN, cw * WIN + WIN)
                        tb0 = (0 if kind == "q" else 2) * CW
                        ct = rott[:, tb0 + cw * WIN:tb0 + cw * WIN + WIN]
                        st_ = rott[:, tb0 + CW + cw * WIN:tb0 + CW + cw * WIN + WIN]

                        S.step("vector", lambda ve, qs_=qs_, scoff=scoff:
                               ve.scalar_tensor_tensor(out=qs_, in0=qs_,
                                                       scalar=sv[:, scoff:scoff + 1],
                                                       in1=PS[:, 0:WIN],
                                                       op0=ALU.mult, op1=ALU.mult))
                        S.wait("tensor", "vector")
                        S.step("tensor", lambda te, qs_=qs_:
                               te.matmul(PT[:, 0:WIN], permt[:, :], qs_,
                                         start=True, stop=True))
                        S.wait("vector", "tensor")
                        S.seq("vector", [
                            lambda ve, qs_=qs_, ct=ct: ve.tensor_mul(
                                out=t1b[:, 0:WIN], in0=qs_, in1=ct),
                            lambda ve, st_=st_: ve.tensor_mul(
                                out=sq2f[:, 0:WIN], in0=PT[:, 0:WIN], in1=st_),
                            lambda ve, qs_=qs_: ve.tensor_add(
                                out=qs_, in0=t1b[:, 0:WIN], in1=sq2f[:, 0:WIN]),
                        ])

            if l == 0:
                S.wait("vector", "tensor")
                S.wait("vector", "vector", S.c["vector"])

                def _xhc(ve, cwl=whi - 1 - wlo):
                    for g in range(NG):
                        r = ve.tensor_copy(out=colc[:, 3 * NG + g:3 * NG + g + 1],
                                       in_=xn_c[:, g * (CW + 1) + 1 + cwl * WIN + WIN - 1:
                                                g * (CW + 1) + 1 + cwl * WIN + WIN])
                    return r
                S.step("vector", _xhc)

            for w in wsB:
                cw = w - wlo
                usemw0 = (w == W0)
                for ic in range(NIC):
                  for h2 in range(2):
                    h = 2 * ic + h2
                    prow = h2 * 64
                    for qt in range(NQT):
                        if usemw0:
                            S.wait("sync", "vector")
                            S.dma("sync", mw0s[:, 0:640],
                                  mask_w0[qt * 128:(qt + 1) * 128, qt * 128:qt * 128 + 640])
                            mld = S.c["d_sy"]
                            S.drain("sync")
                        S.wait("tensor", "vector")
                        S.wait("tensor", "scalar")

                        def _sim(te, l=l, cw=cw, qt=qt, ic=ic, prow=prow):
                            q_ = q_sl("q", ic, cw * WIN + qt * 128,
                                      cw * WIN + (qt + 1) * 128)[prow:prow + 64, :]
                            for j in range(qt, qt + 5):
                                if j < NQT:
                                    if cw == 0:
                                        k_ = kp[prow:prow + 64,
                                                (l * NIC + ic) * WIN + j * 128:
                                                (l * NIC + ic) * WIN + (j + 1) * 128]
                                    else:
                                        k_ = q_sl("k", ic, (cw - 1) * WIN + j * 128,
                                                  (cw - 1) * WIN + (j + 1) * 128)[prow:prow + 64, :]
                                else:
                                    k_ = q_sl("k", ic, cw * WIN + (j - NQT) * 128,
                                              cw * WIN + (j - NQT + 1) * 128)[prow:prow + 64, :]
                                r = te.matmul(PB[:, (j - qt) * 128:(j - qt + 1) * 128],
                                              q_, k_, start=True, stop=True)
                            return r
                        S.step("tensor", _sim)
                        S.wait("vector", "tensor")
                        if usemw0:
                            S.wait_dma("vector", "d_sy", mld)

                        if usemw0:
                            _ma = [lambda ve: ve.tensor_add(out=PB[:, 0:640], in0=PB[:, 0:640],
                                                            in1=mw0s[:, 0:640])]
                        else:
                            _ma = [lambda ve: (ve.tensor_add(out=PB[:, 0:128], in0=PB[:, 0:128],
                                                             in1=cons_f[:, 0:128]),
                                               ve.tensor_add(out=PB[:, 512:640],
                                                             in0=PB[:, 512:640],
                                                             in1=cons_f[:, 128:256]))[1]]
                        S.seq("vector", _ma + [
                            lambda ve: ve.reduce_max(out=essf[:, 0:1], in_=PB[:, 0:640],
                                                     axis=AX, negate=True)])
                        S.wait("scalar", "vector")
                        S.step("scalar", lambda se: se.activation(
                            out=ebuf[:, 0:640], in_=PB[:, 0:640], func=AF.Exp,
                            bias=essf[:, 0:1], accum_out=essf[:, 1:2]))
                        S.wait("vector", "scalar")

                        S.seq("vector", [
                            lambda ve: ve.reciprocal(out=essf[:, 2:3], in_=essf[:, 1:2]),
                            lambda ve: ve.tensor_scalar_mul(out=ebuf[:, 0:640],
                                                            in0=ebuf[:, 0:640],
                                                            scalar1=essf[:, 2:3]),
                        ])
                        S.wait("tensor", "vector")
                        S.wait("tensor", "scalar")

                        def _tp(te):
                            ptv = PT[:, :].bitcast(BF16)
                            for j in range(5):
                                r = te.transpose(ptv[:, j * 128:(j + 1) * 128],
                                                 ebuf[:, j * 128:(j + 1) * 128], ident[:, :])
                            return r
                        S.step("tensor", _tp)
                        S.wait("scalar", "tensor")
                        S.step("scalar", lambda se: se.copy(
                            out=ptb[:, 0:640], in_=PT[:, :].bitcast(BF16)[:, 0:640]))
                        S.wait("tensor", "scalar")

                        def _pv(te, l=l, cw=cw, qt=qt, h=h, prow=prow):
                            for j in range(qt, qt + 5):
                                if j < NQT:
                                    if cw == 0:
                                        v_ = vp[:, l * NQT * INNER + j * INNER + h * 64:
                                                l * NQT * INNER + j * INNER + (h + 1) * 64]
                                    else:
                                        v_ = vtm[:, ((cw - 1) * NQT + j) * 512 + h * 64:
                                                 ((cw - 1) * NQT + j) * 512 + (h + 1) * 64]
                                else:
                                    v_ = vtm[:, (cw * NQT + j - NQT) * 512 + h * 64:
                                             (cw * NQT + j - NQT) * 512 + (h + 1) * 64]
                                r = te.matmul(PS[prow:prow + 64, qt * 128:(qt + 1) * 128], v_,
                                              ptb[:, (j - qt) * 128:(j - qt + 1) * 128],
                                              start=(j == qt), stop=(j == qt + 4),
                                              tile_position=(0, prow))
                            return r
                        S.step("tensor", _pv)
                  S.wait("scalar", "tensor")
                  S.step("scalar", lambda se, ic=ic:
                         se.copy(out=o1b[:, ic * WIN:(ic + 1) * WIN], in_=PS[:, 0:WIN]))

                for oc in range(NG):
                    pb = PA[oc % 2]
                    S.wait("tensor", "scalar")
                    S.wait("tensor", "vector")

                    def _wo2(te, oc=oc, pb=pb):
                        for ic in range(NIC):
                            r = te.matmul(pb[:, 0:WIN],
                                          wuni[:, S2 + ic * C + oc * 128:S2 + ic * C + (oc + 1) * 128],
                                          o1b[:, ic * WIN:(ic + 1) * WIN],
                                          start=(ic == 0), stop=(ic == NIC - 1))
                        return r
                    S.step("tensor", _wo2)
                    S.wait("vector", "tensor")

                    def _xm1(ve, oc=oc, pb=pb, cw=cw, l=l):
                        dst = str_sl(oc, cw * WIN, cw * WIN + WIN)
                        if l == 0:
                            src1 = xn_c[:, oc * (CW + 1) + 1 + cw * WIN:
                                        oc * (CW + 1) + 1 + cw * WIN + WIN]
                            return ve.tensor_add(out=dst, in0=pb[:, 0:WIN], in1=src1)
                        else:
                            return ve.tensor_add(out=dst, in0=pb[:, 0:WIN], in1=dst)
                    S.step("vector", _xm1)

            if whi - 1 >= FA[l]:
                cwl = whi - 1 - wlo
                S.wait("vector", "tensor")
                S.wait("vector", "scalar")
                S.wait("vector", "vector", S.c["vector"])

                def _kvc(ve, cwl=cwl, l=l):
                    for ic in range(NIC):
                        ve.tensor_copy(out=kp[:, (l * NIC + ic) * WIN:(l * NIC + ic + 1) * WIN],
                                       in_=q_sl("k", ic, cwl * WIN, cwl * WIN + WIN))
                    for tc in range(NQT):
                        r = ve.tensor_copy(out=vp[:, l * NQT * INNER + tc * INNER:
                                                  l * NQT * INNER + (tc + 1) * INNER],
                                           in_=vtm[:, (cwl * NQT + tc) * 512:(cwl * NQT + tc + 1) * 512])
                    return r
                S.step("vector", _kvc)

            if (whi - 1 >= FB[l]) and (whi - 1 < FC[l]):
                S.wait("vector", "tensor")
                S.wait("vector", "vector", S.c["vector"])

                def _mc0(ve, cwl=whi - 1 - wlo, l=l):
                    for g in range(NG):
                        r = ve.tensor_copy(out=colc[:, (0 if l == 0 else 2) * NG + g:
                                                (0 if l == 0 else 2) * NG + g + 1],
                                       in_=str_sl(g, cwl * WIN + WIN - 1, cwl * WIN + WIN))
                    return r
                S.step("vector", _mc0)

            for w in wsC:
                cw = w - wlo
                if w == FC[l] and w - 1 >= wlo and w - 1 < FC[l]:
                    S.wait("vector", "tensor")
                    S.wait("vector", "vector", S.c["vector"])

                    def _mc1(ve, cwp=cw - 1, l=l):
                        for g in range(NG):
                            r = ve.tensor_copy(out=colc[:, (0 if l == 0 else 2) * NG + g:
                                                    (0 if l == 0 else 2) * NG + g + 1],
                                           in_=str_sl(g, cwp * WIN + WIN - 1, cwp * WIN + WIN))
                        return r
                    S.step("vector", _mc1)

                S.wait("tensor", "vector")
                ccol = (0 if l == 0 else 2) * NG
                for g in range(NG):
                    S.wait("scalar", "tensor")
                    S.wait("scalar", "vector")

                    def _sqc(se, g=g, cw=cw, ccol=ccol):
                        if g < 4:
                            return se.activation(out=sq2f[:, 0:WIN],
                                                 in_=str_sl(g, cw * WIN, cw * WIN + WIN),
                                                 func=AF.Square)
                        se.activation(out=sq2f[:, 1:WIN],
                                      in_=str_sl(g, cw * WIN, cw * WIN + WIN - 1),
                                      func=AF.Square)
                        return se.activation(out=sq2f[:, 0:1],
                                             in_=colc[:, ccol + g:ccol + g + 1],
                                             func=AF.Square)
                    sc = S.step("scalar", _sqc)
                    S.wait("tensor", "scalar", sc)

                    def _mm2(te, g=g, cw=cw, ccol=ccol):
                        if g < 4:
                            te.matmul(PS[0:1, 0:WIN], ones,
                                      str_sl(g, cw * WIN, cw * WIN + WIN),
                                      start=(g == 0), stop=False)
                        else:
                            te.matmul(PS[0:1, 1:WIN], ones,
                                      str_sl(g, cw * WIN, cw * WIN + WIN - 1),
                                      start=False, stop=False)
                            te.matmul(PS[0:1, 0:1], ones, colc[:, ccol + g:ccol + g + 1],
                                      start=False, stop=(g == NG - 1))
                        r = te.matmul(PB[0:1, 0:WIN], ones, sq2f[:, 0:WIN],
                                      start=(g == 0), stop=(g == NG - 1))
                        return r
                    S.step("tensor", _mm2)

                S.wait("vector", "tensor")

                S.seq("vector", [
                    lambda ve: ve.tensor_scalar(out=strow[0:1, 1024:1024 + WIN],
                                                in0=PS[0:1, 0:WIN], scalar1=1.0 / C,
                                                scalar2=None, op0=ALU.mult),
                    lambda ve: ve.tensor_mul(out=strow[0:1, 0:WIN],
                                             in0=strow[0:1, 1024:1024 + WIN],
                                             in1=strow[0:1, 1024:1024 + WIN]),
                    lambda ve: ve.scalar_tensor_tensor(out=strow[0:1, 0:WIN],
                                                       in0=PB[0:1, 0:WIN], scalar=1.0 / C,
                                                       in1=strow[0:1, 0:WIN],
                                                       op0=ALU.mult, op1=ALU.subtract),
                ])
                S.wait("scalar", "vector")
                S.seq("scalar", [
                    lambda se: se.activation(out=strow[0:1, 0:WIN], in_=strow[0:1, 0:WIN],
                                             func=AF.Ln, bias=cons_f[0:1, 258:259]),
                    lambda se: se.activation(out=strow[0:1, 0:WIN], in_=strow[0:1, 0:WIN],
                                             func=AF.Exp, scale=-0.5),
                ])
                S.wait("vector", "scalar")
                S.step("vector", lambda ve: ve.tensor_mul(
                    out=strow[0:1, WIN:2 * WIN], in0=strow[0:1, 0:WIN],
                    in1=strow[0:1, 1024:1024 + WIN]))
                S.wait("tensor", "vector")

                def _bc2(te):
                    te.matmul(PA[2][:, 0:WIN], bc16[0:1, 0:128],
                              strow[0:1, 0:WIN], start=True, stop=True)
                    return te.matmul(PA[3][:, 0:WIN], bc16[0:1, 0:128],
                                     strow[0:1, WIN:2 * WIN], start=True, stop=True)
                S.step("tensor", _bc2)
                S.wait("vector", "tensor")

                lgo = l * (3 * NG + 2)
                fns = []
                for g in range(NG):
                    lg = sv[:, lgo + g:lgo + g + 1]
                    lb = sv[:, lgo + NG + g:lgo + NG + g + 1]
                    buf2 = t1b[:, (g % 2) * WIN:(g % 2) * WIN + WIN]
                    if g < 4:
                        fns.append(lambda ve, g=g, cw=cw, buf2=buf2: ve.tensor_mul(
                            out=buf2, in0=str_sl(g, cw * WIN, cw * WIN + WIN),
                            in1=PA[2][:, 0:WIN]))
                    else:
                        fns.append(lambda ve, g=g, cw=cw, buf2=buf2, ccol=ccol: (
                            ve.tensor_mul(out=buf2[:, 1:WIN],
                                          in0=str_sl(g, cw * WIN, cw * WIN + WIN - 1),
                                          in1=PA[2][:, 1:WIN]),
                            ve.tensor_mul(out=buf2[:, 0:1],
                                          in0=colc[:, ccol + g:ccol + g + 1],
                                          in1=PA[2][:, 0:1]))[1])
                    fns += [
                        lambda ve, buf2=buf2: ve.tensor_sub(out=buf2, in0=buf2,
                                                            in1=PA[3][:, 0:WIN]),
                        lambda ve, g=g, lg=lg, lb=lb, buf2=buf2: ve.tensor_scalar(
                            out=lno[:, g * WIN:(g + 1) * WIN], in0=buf2,
                            scalar1=lg, scalar2=lb, op0=ALU.mult, op1=ALU.add),
                    ]
                S.seq("vector", fns)

                S.wait("vector", "vector", S.c["vector"])

                def _capw(ve, cw=cw, ccol=ccol):
                    for g in range(NG):
                        r = ve.tensor_copy(out=colc[:, ccol + g:ccol + g + 1],
                                       in_=str_sl(g, cw * WIN + WIN - 1, cw * WIN + WIN))
                    return r
                S.step("vector", _capw)

                for sx in range(6):
                    sbx = sx % 2
                    S.wait("sync", "tensor")
                    for g in range(NG):
                        S.dma("sync",
                              wuni[:, S1 + sbx * 7280 + g * 910:S1 + sbx * 7280 + g * 910 + 455],
                              s_w1[l, g * 128:(g + 1) * 128, sx * SX:(sx + 1) * SX])
                        S.dma("sync",
                              wuni[:, S1 + sbx * 7280 + g * 910 + 455:S1 + sbx * 7280 + (g + 1) * 910],
                              s_w1[l, g * 128:(g + 1) * 128, HID + sx * SX:HID + (sx + 1) * SX])
                    w1ld = S.c["d_sy"]
                    S.drain("sync")
                    S.wait_dma("tensor", "d_sy", w1ld)
                    co = 0
                    for ci, cn in enumerate(SXCH):
                        S.wait("tensor", "vector")
                        S.wait("tensor", "scalar")

                        def _w1m(te, sbx=sbx, co=co, cn=cn):
                            for d in range(NG):
                                te.matmul(PA[0][0:cn, 0:WIN],
                                          wuni[:, S1 + sbx * 7280 + d * 910 + co:
                                               S1 + sbx * 7280 + d * 910 + co + cn],
                                          lno[:, d * WIN:(d + 1) * WIN],
                                          start=(d == 0), stop=(d == NG - 1))
                            for d in range(NG):
                                r = te.matmul(PA[1][0:cn, 0:WIN],
                                              wuni[:, S1 + sbx * 7280 + d * 910 + 455 + co:
                                                   S1 + sbx * 7280 + d * 910 + 455 + co + cn],
                                              lno[:, d * WIN:(d + 1) * WIN],
                                              start=(d == 0), stop=(d == NG - 1))
                            return r
                        S.step("tensor", _w1m)
                        S.wait("scalar", "tensor")
                        S.wait("scalar", "vector")

                        def _gel(se, l=l, sx=sx, ci=ci, cn=cn):
                            bcol = l * 48 + sx * 8
                            se.add(out=sq2f[0:cn, 0:WIN], in_=PA[0][0:cn, 0:WIN],
                                   add=b1s[0:cn, bcol + 2 * ci:bcol + 2 * ci + 1])
                            return se.activation(out=t1b[0:cn, 0:WIN], in_=PA[1][0:cn, 0:WIN],
                                                 func=AF.Gelu,
                                                 bias=b1s[0:cn, bcol + 2 * ci + 1:bcol + 2 * ci + 2])
                        S.step("scalar", _gel)
                        S.wait("vector", "scalar")
                        S.step("vector", lambda ve, sx=sx, ci=ci, cn=cn:
                               ve.tensor_mul(out=agb[0:cn, (sx * 4 + ci) * WIN:(sx * 4 + ci + 1) * WIN],
                                             in0=sq2f[0:cn, 0:WIN], in1=t1b[0:cn, 0:WIN]))
                        co += cn

                for sx in range(6):
                    sbx = sx % 2
                    S.wait("sync", "tensor")
                    co = 0
                    for ci, cn in enumerate(SXCH):
                        S.dma("sync",
                              wuni[0:cn, S2 + sbx * 4096 + ci * C:S2 + sbx * 4096 + (ci + 1) * C],
                              s_w2[l, sx * SX + co:sx * SX + co + cn, :])
                        co += cn
                    w2ld = S.c["d_sy"]
                    S.drain("sync")
                    S.wait_dma("tensor", "d_sy", w2ld)
                    if sx == 0:
                        S.wait("tensor", "vector")

                    def _w2m(te, sx=sx, sbx=sbx):
                        for oc in range(NG):
                            for ci, cn in enumerate(SXCH):
                                r = te.matmul(w2psum[oc],
                                              wuni[0:cn, S2 + sbx * 4096 + ci * C + oc * 128:
                                                   S2 + sbx * 4096 + ci * C + (oc + 1) * 128],
                                              agb[0:cn, (sx * 4 + ci) * WIN:(sx * 4 + ci + 1) * WIN],
                                              start=(sx == 0 and ci == 0),
                                              stop=(sx == 5 and ci == 3))
                        return r
                    S.step("tensor", _w2m)

                S.wait("vector", "tensor")

                def _x1(ve, cw=cw, l=l):
                    for oc in range(NG):
                        b2c = sv[:, l * (3 * NG + 2) + 2 * NG + oc:
                                 l * (3 * NG + 2) + 2 * NG + oc + 1]
                        dst = str_sl(oc, cw * WIN, cw * WIN + WIN)
                        r = ve.scalar_tensor_tensor(out=dst, in0=w2psum[oc], scalar=b2c,
                                                    in1=dst, op0=ALU.add, op1=ALU.add)
                    return r
                S.step("vector", _x1)

                if l == 0 and w == whi - 1:
                    S.wait("vector", "vector", S.c["vector"])

                    def _x1c(ve, cw=cw):
                        for g in range(NG):
                            r = ve.tensor_copy(out=colc[:, 4 * NG + g:4 * NG + g + 1],
                                           in_=str_sl(g, cw * WIN + WIN - 1, cw * WIN + WIN))
                        return r
                    S.step("vector", _x1c)

                if l == 1 and w >= W0:
                    S.wait("gpsimd", "vector")
                    for oc in range(NG):
                        S.dma_out(outT[oc * 128:(oc + 1) * 128,
                                       (w - W0) * WIN:(w - W0 + 1) * WIN],
                                  str_sl(oc, cw * WIN, cw * WIN + WIN))

            S.barrier()

    S.wait_dma("gpsimd", "d_out")

    with es, nc.Block() as block, \
         nc.semaphore("s_te") as s_te, nc.semaphore("s_ve") as s_ve, \
         nc.semaphore("s_se") as s_se, nc.semaphore("s_sy") as s_sy, \
         nc.semaphore("s_ge") as s_ge, nc.semaphore("d_sy") as d_sy, \
         nc.semaphore("d_ge") as d_ge, nc.semaphore("d_out") as d_out, \
         nc.semaphore("d_x0") as d_x0, nc.semaphore("d_x1") as d_x1:
        S.sem.update(dict(tensor=s_te, vector=s_ve, scalar=s_se, sync=s_sy,
                          gpsimd=s_ge, d_sy=d_sy, d_ge=d_ge, d_out=d_out,
                          d_x0=d_x0, d_x1=d_x1))
        block.tensor(lambda te: [f(te) for f in S.q["tensor"]])
        block.vector(lambda ve: [f(ve) for f in S.q["vector"]])
        block.scalar(lambda se: [f(se) for f in S.q["scalar"]])
        block.sync(lambda sy: [f(sy) for f in S.q["sync"]])
        block.gpsimd(lambda ge: [f(ge) for f in S.q["gpsimd"]])
    return nc


def _host_tables(PRE, OWN, n_total, own_starts):
    TB = PRE + OWN
    half = np.arange(0, DH, 2, dtype=np.float64)
    inv_freq = 1.0 / (10000.0 ** (half / DH))
    base = (half + 0.4 * DH) / (1.4 * DH)
    tabs = []
    for s0 in own_starts:
        pos = np.arange(s0 - PRE, s0 - PRE + TB, dtype=np.float64)
        fr = pos[None, :] * inv_freq[:, None]
        power = (pos - n_total // 2) / (WIN // 2)
        scale = base[:, None] ** power[None, :]
        cos = np.cos(fr)
        sin = np.sin(fr)
        cq = np.concatenate([cos * scale, cos * scale], 0) * 8.0
        sq = np.concatenate([-sin * scale, sin * scale], 0) * 8.0
        ck = np.concatenate([cos / scale, cos / scale], 0)
        sk = np.concatenate([-sin / scale, sin / scale], 0)
        t = np.concatenate([np.tile(cq, (2, 1)).reshape(128, TB),
                            np.tile(sq, (2, 1)).reshape(128, TB),
                            np.tile(ck, (2, 1)).reshape(128, TB),
                            np.tile(sk, (2, 1)).reshape(128, TB)], axis=1)
        tabs.append(np.ascontiguousarray(t.astype(ml_dtypes.bfloat16)))
    return tabs


def _mask_w0(own_start):
    q = own_start + np.arange(WIN)[:, None]
    j = own_start - WIN + np.arange(2 * WIN)[None, :]
    masked = (q < j) | (q > j + WIN) | (j < 0)
    return np.where(masked, np.float32(NEGM), np.float32(0.0)).astype(np.float32)


def kernel(L0_attn_x, L0_attn_time_decay, L0_attn_time_first, L0_attn_time_mix,
           L0_attn_Wk, L0_attn_Wv, L0_attn_Wr, L0_attn_Wo,
           L0_ffn_wqkv, L0_ffn_q_scale, L0_ffn_k_scale, L0_ffn_wo,
           L0_ffn_ln_g, L0_ffn_ln_b, L0_ffn_w1, L0_ffn_b1, L0_ffn_w2, L0_ffn_b2):
    B, T, _ = L0_attn_x.shape
    PRE = T // 2
    OWN = T // 2
    n_cores = 8
    nc = build_nc(PRE, OWN)

    f32 = lambda a: np.ascontiguousarray(np.asarray(a, dtype=np.float32))
    x = f32(L0_attn_x)
    wk_t = f32(np.asarray(L0_attn_Wk).T)
    wv_t = f32(np.asarray(L0_attn_Wv).T)
    wr_t = f32(np.asarray(L0_attn_Wr).T)
    wo_t = f32(np.asarray(L0_attn_Wo).T)
    tpar = np.zeros((128, 3 * NG), np.float32)
    tpar[:, 0:NG] = f32(L0_attn_time_decay).reshape(NG, 128).T
    tpar[:, NG:2 * NG] = f32(L0_attn_time_first).reshape(NG, 128).T
    tpar[:, 2 * NG:3 * NG] = f32(L0_attn_time_mix).reshape(NG, 128).T
    sv = np.zeros((128, 2 * (3 * NG + 2)), np.float32)
    for l in range(2):
        b = l * (3 * NG + 2)
        sv[:, b:b + NG] = f32(L0_ffn_ln_g[l]).reshape(NG, 128).T
        sv[:, b + NG:b + 2 * NG] = f32(L0_ffn_ln_b[l]).reshape(NG, 128).T
        sv[:, b + 2 * NG:b + 3 * NG] = f32(L0_ffn_b2[l]).reshape(NG, 128).T
        sv[:, b + 3 * NG:b + 3 * NG + 1] = np.tile(f32(L0_ffn_q_scale[l]), 2)[:, None]
        sv[:, b + 3 * NG + 1:b + 3 * NG + 2] = np.tile(f32(L0_ffn_k_scale[l]), 2)[:, None]
    b1v = f32(L0_ffn_b1)
    b1p = np.zeros((128, 2 * 6 * 8), np.float32)
    for l in range(2):
        for sx in range(6):
            co = 0
            for ci, cn in enumerate(SXCH):
                b1p[0:cn, l * 48 + sx * 8 + 2 * ci] = b1v[l, sx * SX + co:sx * SX + co + cn]
                b1p[0:cn, l * 48 + sx * 8 + 2 * ci + 1] = b1v[l, HID + sx * SX + co:HID + sx * SX + co + cn]
                co += cn
    ident = np.eye(128, dtype=ml_dtypes.bfloat16)
    perm = np.zeros((128, 128), dtype=ml_dtypes.bfloat16)
    for i in range(128):
        j = i + 32 if (i % 64) < 32 else i - 32
        perm[j, i] = 1.0
    bc16 = np.kron(np.eye(16, dtype=np.float32), np.ones((1, 128), np.float32))
    hind = np.zeros((128, NIC * H), np.float32)
    for ic in range(NIC):
        for r in range(128):
            hind[r, ic * H + ic * 2 + r // 64] = 1.0
    hindT = np.zeros((H, INNER), np.float32)
    for pcol in range(INNER):
        hindT[pcol // 64, pcol] = 1.0
    masks = np.zeros((128, 260), np.float32)
    rr, cc2 = np.meshgrid(np.arange(128), np.arange(128), indexing="ij")
    masks[:, 0:128] = np.where(rr <= cc2, 0.0, NEGM)
    masks[:, 128:256] = np.where(rr >= cc2, 0.0, NEGM)
    masks[:, 256] = 1.0
    masks[:, 258] = 1e-5
    masks[:, 259] = 1e-24

    tabs = _host_tables(PRE, OWN, T, [0 if i % 2 == 0 else PRE for i in range(n_cores)])

    shared = dict(wkin=wk_t, wvin=wv_t, wrin=wr_t, woin=wo_t, tparams=tpar,
                  wqkv=f32(L0_ffn_wqkv), wof=f32(L0_ffn_wo), w1=f32(L0_ffn_w1),
                  w2=f32(L0_ffn_w2), svecs=sv, b1p=b1p, ident_in=ident, perm_in=perm,
                  bc16in=bc16, hindin=hind, hindTin=hindT)

    in_maps = []
    for i in range(n_cores):
        b = i // 2
        odd = i % 2 == 1
        xb = np.ascontiguousarray(x[b].T)
        if odd:
            xcore = xb
        else:
            xcore = np.concatenate([np.zeros((C, PRE), np.float32), xb[:, :OWN]], axis=1)
        mk = masks.copy()
        mk[:, 257] = 1.0 if odd else 0.0
        m = dict(shared)
        m.update(xT=np.ascontiguousarray(xcore),
                 masks_in=mk,
                 mask_w0=_mask_w0(PRE if odd else 0),
                 rott_in=tabs[i])
        in_maps.append(m)

    res = run_bass_kernel_spmd(nc, in_maps, core_ids=list(range(n_cores)))
    out = np.empty((B, T, C), np.float32)
    for i in range(n_cores):
        b = i // 2
        sl = slice(PRE, T) if i % 2 == 1 else slice(0, OWN)
        out[b, sl, :] = res.results[i]["outT"].T
    return out


# version 10 (final)
# speedup vs baseline: 1.0914x; 1.0914x over previous
import numpy as np
import ml_dtypes
import concourse.bass as bass
import concourse.mybir as mybir
from concourse.bass_utils import run_bass_kernel_spmd

F32 = mybir.dt.float32
BF16 = mybir.dt.bfloat16
ALU = mybir.AluOpType
AF = mybir.ActivationFunctionType
AX = mybir.AxisListType.X

C = 1024
NG = C // 128
WIN = 512
NQT = 4
DH = 64
H = 8
INNER = 512
NIC = INNER // 128
HID = 2730
SX = 455
SXCH = [128, 128, 128, 71]
KCLAMP = float(np.exp(np.float64(60.0)))
NEGM = -1.0e30
EPSWK = 1e-16

ENGS = ("tensor", "vector", "scalar", "sync", "gpsimd")


class Sched:

    def __init__(self):
        self.q = {e: [] for e in ENGS}
        self.c = dict(tensor=0, vector=0, scalar=0, sync=0, gpsimd=0,
                      d_sy=0, d_ge=0, d_out=0, d_x0=0, d_x1=0, d_pl=0, d_ps=0)
        self.sealed = {}
        self.sem = {}

    def op(self, e, fn):
        self.q[e].append(fn)

    def step(self, e, fn):
        self.c[e] += 1

        def run(eng, fn=fn, k=e):
            fn(eng).then_inc(self.sem[k], 1)
        self.q[e].append(run)
        return self.c[e]

    def mark(self, key):
        v = self.c[key]
        if v > 0 and self.sealed.get(key) != v:
            self.sealed[key] = v

            def seal(eng, key=key, v=v):
                eng.wait_ge(self.sem[key], v)
            self.q[key].append(seal)
        return v

    def wait(self, e, key, val=None):
        v = self.c[key] if val is None else val
        if v <= 0:
            return
        if key in self.q and e != key and v == self.c[key] \
                and self.sealed.get(key) != v:
            self.sealed[key] = v

            def seal(eng, key=key, v=v):
                eng.wait_ge(self.sem[key], v)
            self.q[key].append(seal)

        def run(eng, key=key, v=v):
            eng.wait_ge(self.sem[key], v)
        self.q[e].append(run)

    def dma(self, e, out, in_, slow=False, key=None):
        if key is None:
            key = {"sync": "d_sy", "gpsimd": "d_ge"}[e]
        self.c[key] += 1

        def run(eng, out=out, in_=in_, key=key, slow=slow):
            eng.dma_start(out=out, in_=in_,
                          allow_slow_non_contiguous=slow).then_inc(self.sem[key], 16)
        self.q[e].append(run)
        return self.c[key]

    def dma_out(self, out, in_):
        self.c["d_out"] += 1

        def run(eng, out=out, in_=in_):
            eng.dma_start(out=out, in_=in_, accum_op=ALU.add).then_inc(self.sem["d_out"], 16)
        self.q["gpsimd"].append(run)
        return self.c["d_out"]

    def wait_dma(self, e, key, val=None):
        v = self.c[key] if val is None else val
        if v <= 0:
            return

        def run(eng, key=key, v=v):
            eng.wait_ge(self.sem[key], 16 * v)
        self.q[e].append(run)

    def drain(self, e, key=None):
        if key is None:
            key = {"sync": "d_sy", "gpsimd": "d_ge"}[e]
        self.wait_dma(e, key)

    def seq(self, e, fns):
        for i, fn in enumerate(fns):
            if i > 0:
                self.wait(e, e, self.c[e])
            self.step(e, fn)
        return self.c[e]

    def bump(self, e):
        self.c[e] += 1

        def run(eng, k=e):
            eng.nop().then_inc(self.sem[k], 1)
        self.q[e].append(run)
        return self.c[e]

    def barrier(self):
        self.wait_dma("sync", "d_sy")
        self.wait_dma("sync", "d_x0")
        self.wait_dma("sync", "d_x1")
        self.wait_dma("sync", "d_ps")
        self.wait_dma("gpsimd", "d_ge")
        self.wait_dma("gpsimd", "d_out")
        self.wait_dma("gpsimd", "d_pl")
        tgt = {e: self.bump(e) for e in ENGS}
        for e in ENGS:
            for o in ENGS:
                if o != e:
                    self.wait(e, o, tgt[o])


def build_nc(PRE, OWN):
    NT = (PRE + OWN) // WIN
    W0 = PRE // WIN
    TB = PRE + OWN
    NCH = NT // 2
    assert NT % 2 == 0 and W0 >= 5 and W0 % 2 == 0 and NT <= 16

    nc = bass.Bass("TRN2", target_bir_lowering=False)
    dt_ = nc.dram_tensor

    xT = dt_("xT", [C, TB], F32, kind="ExternalInput").ap()
    outT = dt_("outT", [C, OWN], F32, kind="ExternalOutput").ap()
    wkin = dt_("wkin", [C, C], F32, kind="ExternalInput").ap()
    wvin = dt_("wvin", [C, C], F32, kind="ExternalInput").ap()
    wrin = dt_("wrin", [C, C], F32, kind="ExternalInput").ap()
    woin = dt_("woin", [C, C], F32, kind="ExternalInput").ap()
    tparams = dt_("tparams", [128, 3 * NG], F32, kind="ExternalInput").ap()
    wqkv = dt_("wqkv", [2, C, 3 * INNER], F32, kind="ExternalInput").ap()
    wof = dt_("wof", [2, INNER, C], F32, kind="ExternalInput").ap()
    w1 = dt_("w1", [2, C, 2 * HID], F32, kind="ExternalInput").ap()
    w2 = dt_("w2", [2, HID, C], F32, kind="ExternalInput").ap()
    svecs = dt_("svecs", [128, 2 * (3 * NG + 2)], F32, kind="ExternalInput").ap()
    b1p = dt_("b1p", [128, 2 * 6 * 8], F32, kind="ExternalInput").ap()
    ident_in = dt_("ident_in", [128, 128], BF16, kind="ExternalInput").ap()
    perm_in = dt_("perm_in", [128, 128], BF16, kind="ExternalInput").ap()
    bc16in = dt_("bc16in", [16, 16 * 128], F32, kind="ExternalInput").ap()
    hindin = dt_("hindin", [128, NIC * H], F32, kind="ExternalInput").ap()
    hindTin = dt_("hindTin", [H, INNER], F32, kind="ExternalInput").ap()
    masks_in = dt_("masks_in", [128, 260], F32, kind="ExternalInput").ap()
    mask_w0 = dt_("mask_w0", [WIN, 2 * WIN], F32, kind="ExternalInput").ap()
    rott_in = dt_("rott_in", [128, 4 * TB], BF16, kind="ExternalInput").ap()

    s_wqkv = dt_("s_wqkv", [2, C, 3 * INNER], BF16, kind="Internal").ap()
    s_wof = dt_("s_wof", [2, INNER, C], BF16, kind="Internal").ap()
    s_w1 = dt_("s_w1", [2, C, 2 * HID], BF16, kind="Internal").ap()
    s_w2 = dt_("s_w2", [2, HID, C], BF16, kind="Internal").ap()

    from contextlib import ExitStack
    es = ExitStack()
    sb = lambda n, s, d: es.enter_context(nc.sbuf_tensor(n, s, d))
    pst = lambda n, s, d: es.enter_context(nc.psum_tensor(n, s, d))

    wuni = sb("wuni", [128, 24576], BF16)
    S1 = 0
    S2 = 16384
    uA = sb("uA", [128, 8192], F32)
    uC = sb("uC", [128, 22784], BF16)
    xn_c = sb("xn_c", [128, NG * (2 * WIN + 1)], BF16)
    agxw = sb("agxw", [128, 12288], BF16)
    kp = sb("kp", [128, 2 * NIC * WIN], BF16)
    vp = sb("vp", [128, 2 * NQT * INNER], BF16)
    stats = sb("stats", [16, 2 * WIN], F32)
    strow = sb("strow", [1, 2048], F32)
    bc16 = sb("bc16", [16, 16 * 128], F32)
    hind = sb("hind", [128, NIC * H], F32)
    hindT = sb("hindT", [H, INNER], F32)
    cons_f = sb("cons_f", [128, 260], F32)
    ident = sb("ident", [128, 128], BF16)
    permt = sb("permt", [128, 128], BF16)
    tp = sb("tp", [128, 3 * NG], F32)
    dec = sb("dec", [128, 2 * NG], F32)
    sv = sb("sv", [128, 2 * (3 * NG + 2)], F32)
    b1s = sb("b1s", [128, 2 * 6 * 8], F32)
    scst = sb("scst", [128, 2 * NG], F32)
    xprevc = sb("xprevc", [128, NG], F32)
    colc = sb("colc", [128, 5 * NG], F32)
    essf = sb("essf", [128, 8], F32)
    rn8 = sb("rn8", [H, WIN], F32)

    R_xmt, R_rwkv, R_sig = 0, 4096, 8192
    R_kx, R_vv, R_skv = 8704, 9728, 10752
    R_wkv, R_rcp, R_sqs, R_aot = 12804, 14852, 16900, 18948
    F_qk, F_vtm, F_lno, F_o1 = 0, 8192, 12288, 16384
    F_eb, F_pt, F_t1, F_sq2 = 18432, 19072, 19712, 21760
    F_mw = F_t1
    assert F_sq2 + 1024 <= 22784 and R_aot + 1024 <= 22784

    PA = [pst(f"PA{i}", [128, 512], F32) for i in range(4)]
    PB = pst("PB", [128, 1024], F32)
    PS = pst("PS", [128, 512], F32)
    PT = pst("PT", [128, 512], F32)

    S = Sched()
    ones = cons_f[:, 256:257]
    smask = cons_f[:, 257:258]

    S.dma("sync", cons_f[:, :], masks_in)
    S.dma("sync", ident[:, :], ident_in)
    S.dma("sync", permt[:, :], perm_in)
    S.dma("sync", bc16[:, :], bc16in)
    S.dma("sync", hind[:, :], hindin)
    S.dma("sync", hindT[:, :], hindTin)
    S.dma("sync", tp[:, :], tparams)
    S.dma("sync", sv[:, :], svecs)
    S.dma("sync", b1s[:, :], b1p)
    for g in range(NG):
        S.dma("gpsimd", wuni[:, g * C:(g + 1) * C], wkin[g * 128:(g + 1) * 128, :])
        S.dma("gpsimd", wuni[:, 8192 + g * C:8192 + (g + 1) * C], wvin[g * 128:(g + 1) * 128, :])
    for j0 in range(0, OWN, 2048):
        j1 = min(j0 + 2048, OWN)
        S.dma_out(outT[:, j0:j1], xT[:, PRE + j0:PRE + j1])

    pieces = []
    for l in range(2):
        for (src, dst, rows) in ((wqkv[l], s_wqkv[l], C), (wof[l], s_wof[l], INNER),
                                 (w1[l], s_w1[l], C), (w2[l], s_w2[l], HID)):
            for r0 in range(0, rows, 128):
                r1 = min(r0 + 128, rows)
                pieces.append((src[r0:r1, :], dst[r0:r1, :], r1 - r0, src.shape[1]))
    pstate = [0]

    def emit_pieces(k):
        while k > 0 and pstate[0] < len(pieces):
            i = pstate[0]
            src, dst, rr, cc = pieces[i]
            slot = i % 2
            ssl = agxw[0:rr, slot * 6144: slot * 6144 + cc]
            if i >= 2:
                S.wait_dma("gpsimd", "d_ps", i - 1)
            S.dma("gpsimd", ssl, src, key="d_pl")
            S.wait_dma("sync", "d_pl", i + 1)
            S.dma("sync", dst, ssl, key="d_ps")
            pstate[0] += 1
            k -= 1

    S.wait_dma("scalar", "d_sy")

    k1 = S.step("scalar", lambda se: se.activation(out=dec[:, 0:NG], in_=tp[:, 0:NG],
                                                   func=AF.Exp))
    S.wait("scalar", "scalar", k1)
    S.step("scalar", lambda se: se.activation(out=dec[:, 0:NG], in_=dec[:, 0:NG],
                                              func=AF.Exp, scale=-1.0))
    S.step("scalar", lambda se: se.activation(out=dec[:, NG:2 * NG], in_=tp[:, NG:2 * NG],
                                              func=AF.Exp))

    def _z(ve):
        ve.memset(scst[:, :], 0.0)
        ve.memset(xprevc[:, :], 0.0)
        ve.memset(stats[:, :], 0.0)
        ve.memset(strow[:, :], 0.0)
        return ve.memset(colc[:, :], 0.0)
    S.step("vector", _z)
    S.barrier()

    f32v = lambda off, n: uC[:, off:off + 2 * n].bitcast(F32)
    kxb = f32v(R_kx, 512)
    vvb = f32v(R_vv, 512)
    skv = f32v(R_skv, 1026)
    wkvb = f32v(R_wkv, 1024)
    rcpb = f32v(R_rcp, 1024)
    sqs2 = f32v(R_sqs, 1024)
    aot = f32v(R_aot, 512)
    aot2 = f32v(R_aot + 1024, 512)
    xmt = uC[:, R_xmt:R_xmt + NG * WIN]
    rwkvb = uC[:, R_rwkv:R_rwkv + NG * WIN]
    sigb = uC[:, R_sig:R_sig + WIN]

    last_xuse = [0, 0]
    strow_free = 0
    mm_prev = [0, 0]
    pa01_free_s = 0
    pt_free_s = 0
    for t in range(NT):
        own = t >= W0
        buf = t % 2
        xts = uA[:, buf * 4096:buf * 4096 + NG * WIN]
        S.wait("sync", "vector", last_xuse[buf])
        xkey = f"d_x{buf}"
        for g in range(NG):
            S.dma("sync", xts[:, g * WIN:(g + 1) * WIN],
                  xT[g * 128:(g + 1) * 128, t * WIN:(t + 1) * WIN], key=xkey)
        S.drain("sync", xkey)
        S.wait_dma("scalar", xkey)
        S.wait_dma("tensor", xkey)
        S.wait("tensor", "vector", strow_free)
        for g in range(NG):
            sq = sqs2[:, (g % 2) * 512:(g % 2) * 512 + 512]
            S.wait("scalar", "tensor", mm_prev[g % 2])
            sc = S.step("scalar", lambda se, g=g, sq=sq, xts=xts:
                        se.activation(out=sq, in_=xts[:, g * WIN:(g + 1) * WIN], func=AF.Square))
            S.wait("tensor", "scalar", sc)

            def _mm(te, g=g, sq=sq, xts=xts):
                te.matmul(PS[0:1, 0:WIN], ones, xts[:, g * WIN:(g + 1) * WIN],
                          start=(g == 0), stop=(g == NG - 1))
                return te.matmul(PB[0:1, 0:WIN], ones, sq,
                                 start=(g == 0), stop=(g == NG - 1))
            mm_prev[g % 2] = S.step("tensor", _mm)
            S.mark("tensor")

        S.wait("vector", "tensor")
        S.seq("vector", [
            lambda ve: ve.tensor_scalar(out=strow[0:1, 1024:1024 + WIN], in0=PS[0:1, 0:WIN],
                                        scalar1=1.0 / C, scalar2=None, op0=ALU.mult),
            lambda ve: ve.tensor_mul(out=strow[0:1, 0:WIN],
                                     in0=strow[0:1, 1024:1024 + WIN],
                                     in1=strow[0:1, 1024:1024 + WIN]),
            lambda ve: ve.scalar_tensor_tensor(out=strow[0:1, 0:WIN],
                                               in0=PB[0:1, 0:WIN], scalar=1.0 / C,
                                               in1=strow[0:1, 0:WIN],
                                               op0=ALU.mult, op1=ALU.subtract),
        ])
        strow_free = S.mark("vector")
        S.wait("scalar", "vector")
        S.seq("scalar", [
            lambda se: se.activation(out=strow[0:1, 0:WIN], in_=strow[0:1, 0:WIN],
                                     func=AF.Ln, bias=cons_f[0:1, 258:259]),
            lambda se: se.activation(out=strow[0:1, 0:WIN], in_=strow[0:1, 0:WIN],
                                     func=AF.Exp, scale=-0.5),
        ])
        S.wait("vector", "scalar")
        S.step("vector", lambda ve: ve.tensor_mul(
            out=strow[0:1, WIN:2 * WIN], in0=strow[0:1, 0:WIN],
            in1=strow[0:1, 1024:1024 + WIN]))
        S.wait("sync", "vector")
        S.dma("sync", stats[t:t + 1, 0:2 * WIN], strow[0:1, 0:2 * WIN])
        S.drain("sync")
        S.wait_dma("tensor", "d_sy")
        S.wait("tensor", "vector")

        def _bc(te, t=t):
            te.matmul(PA[2][:, 0:WIN], bc16[:, t * 128:(t + 1) * 128],
                      stats[:, 0:WIN], start=True, stop=True)
            return te.matmul(PA[3][:, 0:WIN], bc16[:, t * 128:(t + 1) * 128],
                             stats[:, WIN:2 * WIN], start=True, stop=True)
        S.step("tensor", _bc)
        S.wait("vector", "tensor")
        fns = []
        for g in range(NG):
            xg = xts[:, g * WIN:(g + 1) * WIN]
            xn = sqs2[:, 0:WIN]
            tmv = tp[:, 2 * NG + g:2 * NG + g + 1]
            xmg = xmt[:, g * WIN:(g + 1) * WIN]
            fns += [
                lambda ve, xg=xg, xn=xn: ve.tensor_mul(out=xn, in0=xg, in1=PA[2][:, 0:WIN]),
                lambda ve, xn=xn: ve.tensor_sub(out=xn, in0=xn, in1=PA[3][:, 0:WIN]),
                lambda ve, xn=xn: ve.tensor_sub(out=sqs2[:, 512:512 + WIN - 1],
                                                in0=xn[:, 1:WIN], in1=xn[:, 0:WIN - 1]),
                lambda ve, xn=xn, g=g: ve.tensor_sub(out=sqs2[:, 512 + WIN - 1:512 + WIN],
                                                     in0=xn[:, 0:1], in1=xprevc[:, g:g + 1]),
                lambda ve, xmg=xmg, tmv=tmv, xn=xn: ve.scalar_tensor_tensor(
                    out=xmg[:, 1:WIN], in0=sqs2[:, 512:512 + WIN - 1], scalar=tmv,
                    in1=xn[:, 0:WIN - 1], op0=ALU.mult, op1=ALU.add),
                lambda ve, xmg=xmg, tmv=tmv, g=g: ve.scalar_tensor_tensor(
                    out=xmg[:, 0:1], in0=sqs2[:, 512 + WIN - 1:512 + WIN], scalar=tmv,
                    in1=xprevc[:, g:g + 1], op0=ALU.mult, op1=ALU.add),
                lambda ve, xn=xn, g=g: ve.tensor_copy(out=xprevc[:, g:g + 1],
                                                      in_=xn[:, WIN - 1:WIN]),
            ]
        S.seq("vector", fns)
        last_xuse[buf] = S.mark("vector")
        xm_end = last_xuse[buf]

        if own:
            S.wait("gpsimd", "tensor")
            for g in range(NG):
                S.dma("gpsimd", wuni[:, 16384 + g * C:16384 + (g + 1) * C],
                      wrin[g * 128:(g + 1) * 128, :])
            wr_ld = S.c["d_ge"]
            S.drain("gpsimd")

        for g in range(NG):
            S.wait("tensor", "vector", xm_end)
            S.wait("tensor", "scalar", pa01_free_s)

            def _kvm(te, g=g):
                for d in range(NG):
                    te.matmul(PA[0][:, 0:WIN], wuni[:, d * C + g * 128:d * C + (g + 1) * 128],
                              xmt[:, d * WIN:(d + 1) * WIN], start=(d == 0), stop=(d == NG - 1))
                for d in range(NG):
                    r = te.matmul(PA[1][:, 0:WIN],
                                  wuni[:, 8192 + d * C + g * 128:8192 + d * C + (g + 1) * 128],
                                  xmt[:, d * WIN:(d + 1) * WIN], start=(d == 0), stop=(d == NG - 1))
                return r
            S.step("tensor", _kvm)
            S.wait("scalar", "tensor")
            S.wait("scalar", "vector")

            def _ev(se):
                se.activation(out=kxb[:, :], in_=PA[0][:, 0:WIN], func=AF.Exp)
                return se.copy(out=vvb[:, :], in_=PA[1][:, 0:WIN])
            S.step("scalar", _ev)
            pa01_free_s = S.mark("scalar")
            S.wait("vector", "scalar")
            fw = dec[:, NG + g:NG + g + 1]
            fns = [
                lambda ve: ve.tensor_scalar_min(out=kxb[:, :], in0=kxb[:, :], scalar1=KCLAMP),
                lambda ve: ve.tensor_mul(out=vvb[:, :], in0=kxb[:, :], in1=vvb[:, :]),
            ]
            if t == W0:
                fns += [
                    lambda ve, g=g: ve.tensor_mul(out=scst[:, g:g + 1],
                                                  in0=scst[:, g:g + 1], in1=smask),
                    lambda ve, g=g: ve.tensor_mul(out=scst[:, NG + g:NG + g + 1],
                                                  in0=scst[:, NG + g:NG + g + 1], in1=smask),
                ]
            fns += [
                lambda ve, g=g: ve.tensor_copy(out=skv[:, 0:1], in_=scst[:, g:g + 1]),
                lambda ve, g=g: ve.tensor_tensor_scan(
                    out=skv[:, 1:WIN + 1], data0=dec[:, g:g + 1].broadcast_to([128, WIN]),
                    data1=vvb[:, :], initial=scst[:, g:g + 1], op0=ALU.mult, op1=ALU.add),
                lambda ve, g=g: ve.tensor_copy(out=scst[:, g:g + 1], in_=skv[:, WIN:WIN + 1]),
                lambda ve, g=g: ve.tensor_copy(out=skv[:, WIN + 1:WIN + 2],
                                               in_=scst[:, NG + g:NG + g + 1]),
                lambda ve, g=g: ve.tensor_tensor_scan(
                    out=skv[:, WIN + 2:2 * WIN + 2],
                    data0=dec[:, g:g + 1].broadcast_to([128, WIN]),
                    data1=kxb[:, :], initial=scst[:, NG + g:NG + g + 1],
                    op0=ALU.mult, op1=ALU.add),
                lambda ve, g=g: ve.tensor_copy(out=scst[:, NG + g:NG + g + 1],
                                               in_=skv[:, 2 * WIN + 1:2 * WIN + 2]),
            ]
            if own:
                fns += [
                    lambda ve, fw=fw: ve.scalar_tensor_tensor(
                        out=wkvb[:, 0:WIN], in0=vvb[:, :], scalar=fw,
                        in1=skv[:, 0:WIN], op0=ALU.mult, op1=ALU.add),
                    lambda ve, fw=fw: ve.scalar_tensor_tensor(
                        out=wkvb[:, WIN:2 * WIN], in0=kxb[:, :], scalar=fw,
                        in1=skv[:, WIN + 1:2 * WIN + 1], op0=ALU.mult, op1=ALU.add),
                    lambda ve: ve.tensor_scalar_add(out=wkvb[:, WIN:2 * WIN],
                                                    in0=wkvb[:, WIN:2 * WIN], scalar1=EPSWK),
                ]
            S.seq("vector", fns)

            if own:
                if g == 0:
                    S.wait_dma("tensor", "d_ge", wr_ld)
                S.wait("tensor", "scalar", pt_free_s)

                def _r(te, g=g):
                    for d in range(NG):
                        r = te.matmul(PT[:, 0:WIN],
                                      wuni[:, 16384 + d * C + g * 128:16384 + d * C + (g + 1) * 128],
                                      xmt[:, d * WIN:(d + 1) * WIN], start=(d == 0),
                                      stop=(d == NG - 1))
                    return r
                S.step("tensor", _r)
                S.wait("scalar", "tensor")
                S.wait("scalar", "vector")
                S.step("scalar", lambda se: se.activation(out=sqs2[:, 0:WIN],
                                                          in_=PT[:, 0:WIN],
                                                          func=AF.Exp, scale=-1.0))
                pt_free_s = S.mark("scalar")
                S.wait("vector", "scalar")
                S.step("vector", lambda ve: ve.tensor_scalar_add(
                    out=sqs2[:, 512:1024], in0=sqs2[:, 0:WIN], scalar1=1.0))
                S.wait("scalar", "vector")
                S.seq("scalar", [
                    lambda se: se.activation(out=rcpb[:, WIN:2 * WIN],
                                             in_=sqs2[:, 512:1024], func=AF.Ln),
                    lambda se: se.activation(out=sigb[:, :], in_=rcpb[:, WIN:2 * WIN],
                                             func=AF.Exp, scale=-1.0),
                    lambda se: se.activation(out=rcpb[:, 0:WIN],
                                             in_=wkvb[:, WIN:2 * WIN], func=AF.Ln),
                    lambda se: se.activation(out=rcpb[:, 0:WIN], in_=rcpb[:, 0:WIN],
                                             func=AF.Exp, scale=-1.0),
                ])
                S.wait("vector", "scalar")
                S.seq("vector", [
                    lambda ve: ve.tensor_mul(out=wkvb[:, 0:WIN], in0=wkvb[:, 0:WIN],
                                             in1=rcpb[:, 0:WIN]),
                    lambda ve, g=g: ve.tensor_mul(out=rwkvb[:, g * WIN:(g + 1) * WIN],
                                                  in0=wkvb[:, 0:WIN], in1=sigb[:, :]),
                ])

        if own:
            S.wait("gpsimd", "tensor")
            for g in range(NG):
                S.dma("gpsimd", wuni[:, 16384 + g * C:16384 + (g + 1) * C],
                      woin[g * 128:(g + 1) * 128, :])
            wo_ld = S.c["d_ge"]
            S.drain("gpsimd")
            S.wait_dma("tensor", "d_ge", wo_ld)
            S.wait("tensor", "vector")
            base_out = S.c["d_out"]
            wo_cp = {}
            for oc in range(NG):
                pb = PA[oc % 2]
                S.wait("tensor", "scalar", wo_cp.get(oc % 2, pa01_free_s))

                def _wo(te, oc=oc, pb=pb):
                    for g in range(NG):
                        r = te.matmul(pb[:, 0:WIN],
                                      wuni[:, 16384 + g * C + oc * 128:16384 + g * C + (oc + 1) * 128],
                                      rwkvb[:, g * WIN:(g + 1) * WIN], start=(g == 0),
                                      stop=(g == NG - 1))
                    return r
                S.step("tensor", _wo)
                S.wait("scalar", "tensor")
                S.wait_dma("scalar", "d_out", base_out + oc - 1)
                aslot = aot if oc % 2 == 0 else aot2
                S.step("scalar", lambda se, pb=pb, aslot=aslot:
                       se.copy(out=aslot[:, :], in_=pb[:, 0:WIN]))
                wo_cp[oc % 2] = S.mark("scalar")
                S.wait("gpsimd", "scalar")
                S.dma_out(outT[oc * 128:(oc + 1) * 128, (t - W0) * WIN:(t - W0 + 1) * WIN],
                          aslot[:, :])
            pa01_free_s = wo_cp[1]

        if t < 8:
            emit_pieces(11)

    S.barrier()

    FA = {0: W0 - 5, 1: W0 - 2}
    FB = {0: W0 - 4, 1: W0 - 1}
    FC = {0: W0 - 3, 1: W0}
    tt = FA[0] - 1
    xcol = f32v(R_kx, NG + NG)
    for g in range(NG):
        S.dma("sync", xcol[:, g:g + 1],
              xT[g * 128:(g + 1) * 128, (tt + 1) * WIN - 1:(tt + 1) * WIN], slow=True)
    S.drain("sync")
    S.wait_dma("tensor", "d_sy")

    def _bcc(te, tt=tt):
        te.matmul(PS[:, 0:1], bc16[:, tt * 128:(tt + 1) * 128],
                  stats[:, WIN - 1:WIN], start=True, stop=True)
        return te.matmul(PS[:, 1:2], bc16[:, tt * 128:(tt + 1) * 128],
                         stats[:, 2 * WIN - 1:2 * WIN], start=True, stop=True)
    S.step("tensor", _bcc)
    S.wait("vector", "tensor")

    fns = []
    for g in range(NG):
        fns += [
            lambda ve, g=g: ve.tensor_mul(out=xcol[:, NG + g:NG + g + 1],
                                          in0=xcol[:, g:g + 1], in1=PS[:, 0:1]),
            lambda ve, g=g: ve.tensor_sub(out=colc[:, 3 * NG + g:3 * NG + g + 1],
                                          in0=xcol[:, NG + g:NG + g + 1], in1=PS[:, 1:2]),
        ]
    S.seq("vector", fns)
    S.barrier()

    CW = 2 * WIN

    def str_sl(g, c0, c1):
        return uA[:, g * CW + c0:g * CW + c1]

    qkb = uC[:, F_qk:F_qk + 2 * NIC * CW]
    rott = uC[:, F_lno:F_lno + 4 * CW]
    vtm = uC[:, F_vtm:F_vtm + 8 * 512]
    lno = uC[:, F_lno:F_lno + NG * WIN]
    o1b = uC[:, F_o1:F_o1 + NIC * WIN]
    ebuf = uC[:, F_eb:F_eb + 640]
    ptb = uC[:, F_pt:F_pt + 640]
    t1b = uC[:, F_t1:F_t1 + 2048].bitcast(F32)
    sq2f = uC[:, F_sq2:F_sq2 + 1024].bitcast(F32)
    mw0s = uC[:, F_mw:F_mw + 1280].bitcast(F32)
    agb = agxw[:, 0:24 * WIN]
    xwv = agxw[:, 0:NG * 1028].bitcast(F32)

    def q_sl(kind, ic, c0, c1):
        base = 0 if kind == "q" else NIC * CW
        return qkb[:, base + ic * CW + c0:base + ic * CW + c1]

    def inp_sl(g, c0, c1, shifted):
        off = 0 if shifted else 1
        return xn_c[:, g * (CW + 1) + off + c0:g * (CW + 1) + off + c1]

    w2psum = [PA[0][:, 0:512], PA[1][:, 0:512], PA[2][:, 0:512], PA[3][:, 0:512],
              PB[:, 0:512], PB[:, 512:1024], PS[:, 0:512], PT[:, 0:512]]

    for ch in range(NCH):
        wlo, whi = 2 * ch, 2 * ch + 2
        if whi <= FA[0]:
            continue
        if whi > FA[1]:
            S.wait("vector", "vector", S.c["vector"])

            def _hcp(ve):
                for g in range(NG):
                    r = ve.tensor_copy(out=colc[:, 1 * NG + g:1 * NG + g + 1],
                                       in_=colc[:, 4 * NG + g:4 * NG + g + 1])
                return r
            S.step("vector", _hcp)
        if wlo == W0:
            S.wait("vector", "vector", S.c["vector"])
            S.step("vector", lambda ve: ve.tensor_scalar_mul(
                out=colc[:, 0:4 * NG], in0=colc[:, 0:4 * NG], scalar1=smask))
        for l in range(2):
            wsA = [w for w in range(wlo, whi) if w >= FA[l]]
            wsB = [w for w in range(wlo, whi) if w >= FB[l]]
            wsC = [w for w in range(wlo, whi) if w >= FC[l]]
            if not wsA:
                continue

            S.wait("sync", "tensor")
            for g in range(NG):
                S.dma("sync", wuni[:, S1 + g * 1536:S1 + (g + 1) * 1536],
                      s_wqkv[l, g * 128:(g + 1) * 128, :])
            for ic in range(NIC):
                S.dma("sync", wuni[:, S2 + ic * C:S2 + (ic + 1) * C],
                      s_wof[l, ic * 128:(ic + 1) * 128, :])
            S.wait("sync", "vector")
            for ti in range(4):
                S.dma("sync", rott[:, ti * CW:(ti + 1) * CW],
                      rott_in[:, ti * TB + wlo * WIN:ti * TB + whi * WIN])
            qkv_ld = S.c["d_sy"]
            S.drain("sync")

            S.wait("vector", "tensor")

            cw0 = wsA[0] - wlo
            S.wait("vector", "vector", S.c["vector"])

            def _hal(ve, l=l, cw0=cw0):
                src_base = (3 if l == 0 else 1) * NG
                for g in range(NG):
                    r = ve.tensor_copy(
                        out=xn_c[:, g * (CW + 1) + cw0 * WIN:g * (CW + 1) + cw0 * WIN + 1],
                        in_=colc[:, src_base + g:src_base + g + 1])
                return r
            S.step("vector", _hal)

            for w in wsA:
                cw = w - wlo
                if l == 0:
                    S.wait("sync", "tensor")
                    S.wait("sync", "vector")
                    for g in range(NG):
                        S.dma("sync", xwv[:, g * 514:g * 514 + WIN],
                              xT[g * 128:(g + 1) * 128, w * WIN:(w + 1) * WIN])
                    xld = S.c["d_sy"]
                    S.drain("sync")
                    S.wait_dma("tensor", "d_sy", xld)
                    S.wait("tensor", "vector")

                    def _b2(te, w=w):
                        te.matmul(PA[2][:, 0:WIN], bc16[:, w * 128:(w + 1) * 128],
                                  stats[:, 0:WIN], start=True, stop=True)
                        return te.matmul(PA[3][:, 0:WIN], bc16[:, w * 128:(w + 1) * 128],
                                         stats[:, WIN:2 * WIN], start=True, stop=True)
                    S.step("tensor", _b2)
                    S.wait("vector", "tensor")

                    fns = []
                    for g in range(NG):
                        dst = xn_c[:, g * (CW + 1) + 1 + cw * WIN:
                                   g * (CW + 1) + 1 + cw * WIN + WIN]
                        buf2 = t1b[:, (g % 2) * WIN:(g % 2) * WIN + WIN]
                        fns += [
                            lambda ve, g=g, buf2=buf2: ve.tensor_mul(
                                out=buf2, in0=xwv[:, g * 514:g * 514 + WIN],
                                in1=PA[2][:, 0:WIN]),
                            lambda ve, dst=dst, buf2=buf2: ve.tensor_sub(
                                out=dst, in0=buf2, in1=PA[3][:, 0:WIN]),
                        ]
                    S.seq("vector", fns)
                else:
                    S.wait("vector", "gpsimd")
                    S.wait("vector", "vector", S.c["vector"])

                    def _cst(ve, cw=cw):
                        for g in range(NG):
                            dst = xn_c[:, g * (CW + 1) + 1 + cw * WIN:
                                       g * (CW + 1) + 1 + cw * WIN + WIN]
                            r = ve.tensor_copy(out=dst, in_=str_sl(g, cw * WIN, cw * WIN + WIN))
                        return r
                    S.step("vector", _cst)

                S.wait_dma("tensor", "d_sy", qkv_ld)
                S.wait("tensor", "vector")
                for kind, cbase in (("q", 0), ("k", 512)):
                    for ic in range(NIC):
                        pb = PA[ic % 2]
                        S.wait("tensor", "scalar")

                        def _qk(te, cbase=cbase, ic=ic, pb=pb, cw=cw):
                            for d in range(NG):
                                r = te.matmul(pb[:, 0:WIN],
                                              wuni[:, S1 + d * 1536 + cbase + ic * 128:
                                                   S1 + d * 1536 + cbase + (ic + 1) * 128],
                                              inp_sl(d, cw * WIN, cw * WIN + WIN, d >= 4),
                                              start=(d == 0), stop=(d == NG - 1))
                            return r
                        S.step("tensor", _qk)
                        S.wait("scalar", "tensor")
                        S.step("scalar", lambda se, kind=kind, ic=ic, pb=pb, cw=cw:
                               se.copy(out=q_sl(kind, ic, cw * WIN, cw * WIN + WIN),
                                       in_=pb[:, 0:WIN]))
                for tc in range(NQT):
                    pb = PA[2 + (tc % 2)]
                    S.wait("tensor", "scalar")

                    def _v(te, tc=tc, pb=pb, cw=cw):
                        for d in range(NG):
                            r = te.matmul(pb[:, 0:WIN],
                                          inp_sl(d, cw * WIN + tc * 128,
                                                 cw * WIN + (tc + 1) * 128, d >= 4),
                                          wuni[:, S1 + d * 1536 + 1024:S1 + d * 1536 + 1536],
                                          start=(d == 0), stop=(d == NG - 1))
                        return r
                    S.step("tensor", _v)
                    S.wait("scalar", "tensor")
                    S.step("scalar", lambda se, tc=tc, pb=pb, cw=cw:
                           se.copy(out=vtm[:, (cw * NQT + tc) * 512:(cw * NQT + tc + 1) * 512],
                                   in_=pb[:, 0:WIN]))

                for kind in ("q", "k"):
                    scoff = (l * (3 * NG + 2)) + 3 * NG + (0 if kind == "q" else 1)
                    for ic in range(NIC):
                        S.wait("scalar", "tensor")
                        S.wait("scalar", "vector")
                        S.wait("scalar", "scalar", S.c["scalar"])
                        sc = S.step("scalar", lambda se, kind=kind, ic=ic, cw=cw:
                                    se.activation(out=sq2f[:, 0:WIN],
                                                  in_=q_sl(kind, ic, cw * WIN, cw * WIN + WIN),
                                                  func=AF.Square))
                        S.wait("tensor", "scalar", sc)
                        S.step("tensor", lambda te, ic=ic:
                               te.matmul(PS[0:H, 0:WIN], hind[:, ic * H:(ic + 1) * H],
                                         sq2f[:, 0:WIN], start=(ic == 0), stop=(ic == NIC - 1)))
                    S.wait("scalar", "tensor")
                    S.seq("scalar", [
                        lambda se: se.activation(out=rn8[0:H, 0:WIN], in_=PS[0:H, 0:WIN],
                                                 func=AF.Ln, bias=cons_f[0:H, 259:260]),
                        lambda se: se.activation(out=rn8[0:H, 0:WIN], in_=rn8[0:H, 0:WIN],
                                                 func=AF.Exp, scale=-0.5),
                    ])
                    S.wait("vector", "scalar")
                    S.step("vector", lambda ve: ve.tensor_scalar_min(
                        out=rn8[0:H, 0:WIN], in0=rn8[0:H, 0:WIN], scalar1=1e12))
                    for ic in range(NIC):
                        S.wait("tensor", "vector")
                        S.step("tensor", lambda te, ic=ic:
                               te.matmul(PS[:, 0:WIN], hindT[:, ic * 128:(ic + 1) * 128],
                                         rn8[0:H, 0:WIN], start=True, stop=True))
                        S.wait("vector", "tensor")

                        qs_ = q_sl(kind, ic, cw * WIN, cw * WIN + WIN)
                        tb0 = (0 if kind == "q" else 2) * CW
                        ct = rott[:, tb0 + cw * WIN:tb0 + cw * WIN + WIN]
                        st_ = rott[:, tb0 + CW + cw * WIN:tb0 + CW + cw * WIN + WIN]

                        S.step("vector", lambda ve, qs_=qs_, scoff=scoff:
                               ve.scalar_tensor_tensor(out=qs_, in0=qs_,
                                                       scalar=sv[:, scoff:scoff + 1],
                                                       in1=PS[:, 0:WIN],
                                                       op0=ALU.mult, op1=ALU.mult))
                        S.wait("tensor", "vector")
                        S.step("tensor", lambda te, qs_=qs_:
                               te.matmul(PT[:, 0:WIN], permt[:, :], qs_,
                                         start=True, stop=True))
                        S.wait("vector", "tensor")
                        S.seq("vector", [
                            lambda ve, qs_=qs_, ct=ct: ve.tensor_mul(
                                out=t1b[:, 0:WIN], in0=qs_, in1=ct),
                            lambda ve, st_=st_: ve.tensor_mul(
                                out=sq2f[:, 0:WIN], in0=PT[:, 0:WIN], in1=st_),
                            lambda ve, qs_=qs_: ve.tensor_add(
                                out=qs_, in0=t1b[:, 0:WIN], in1=sq2f[:, 0:WIN]),
                        ])

            if l == 0:
                S.wait("vector", "tensor")
                S.wait("vector", "vector", S.c["vector"])

                def _xhc(ve, cwl=whi - 1 - wlo):
                    for g in range(NG):
                        r = ve.tensor_copy(out=colc[:, 3 * NG + g:3 * NG + g + 1],
                                       in_=xn_c[:, g * (CW + 1) + 1 + cwl * WIN + WIN - 1:
                                                g * (CW + 1) + 1 + cwl * WIN + WIN])
                    return r
                S.step("vector", _xhc)

            for w in wsB:
                cw = w - wlo
                usemw0 = (w == W0)
                for ic in range(NIC):
                  for h2 in range(2):
                    h = 2 * ic + h2
                    prow = h2 * 64
                    for qt in range(NQT):
                        if usemw0:
                            S.wait("sync", "vector")
                            S.dma("sync", mw0s[:, 0:640],
                                  mask_w0[qt * 128:(qt + 1) * 128, qt * 128:qt * 128 + 640])
                            mld = S.c["d_sy"]
                            S.drain("sync")
                        S.wait("tensor", "vector")
                        S.wait("tensor", "scalar")

                        def _sim(te, l=l, cw=cw, qt=qt, ic=ic, prow=prow):
                            q_ = q_sl("q", ic, cw * WIN + qt * 128,
                                      cw * WIN + (qt + 1) * 128)[prow:prow + 64, :]
                            for j in range(qt, qt + 5):
                                if j < NQT:
                                    if cw == 0:
                                        k_ = kp[prow:prow + 64,
                                                (l * NIC + ic) * WIN + j * 128:
                                                (l * NIC + ic) * WIN + (j + 1) * 128]
                                    else:
                                        k_ = q_sl("k", ic, (cw - 1) * WIN + j * 128,
                                                  (cw - 1) * WIN + (j + 1) * 128)[prow:prow + 64, :]
                                else:
                                    k_ = q_sl("k", ic, cw * WIN + (j - NQT) * 128,
                                              cw * WIN + (j - NQT + 1) * 128)[prow:prow + 64, :]
                                r = te.matmul(PB[:, (j - qt) * 128:(j - qt + 1) * 128],
                                              q_, k_, start=True, stop=True)
                            return r
                        S.step("tensor", _sim)
                        S.wait("vector", "tensor")
                        if usemw0:
                            S.wait_dma("vector", "d_sy", mld)

                        if usemw0:
                            _ma = [lambda ve: ve.tensor_add(out=PB[:, 0:640], in0=PB[:, 0:640],
                                                            in1=mw0s[:, 0:640])]
                        else:
                            _ma = [lambda ve: (ve.tensor_add(out=PB[:, 0:128], in0=PB[:, 0:128],
                                                             in1=cons_f[:, 0:128]),
                                               ve.tensor_add(out=PB[:, 512:640],
                                                             in0=PB[:, 512:640],
                                                             in1=cons_f[:, 128:256]))[1]]
                        S.seq("vector", _ma + [
                            lambda ve: ve.reduce_max(out=essf[:, 0:1], in_=PB[:, 0:640],
                                                     axis=AX, negate=True)])
                        S.wait("scalar", "vector")
                        S.step("scalar", lambda se: se.activation(
                            out=ebuf[:, 0:640], in_=PB[:, 0:640], func=AF.Exp,
                            bias=essf[:, 0:1], accum_out=essf[:, 1:2]))
                        S.wait("vector", "scalar")

                        S.seq("vector", [
                            lambda ve: ve.reciprocal(out=essf[:, 2:3], in_=essf[:, 1:2]),
                            lambda ve: ve.tensor_scalar_mul(out=ebuf[:, 0:640],
                                                            in0=ebuf[:, 0:640],
                                                            scalar1=essf[:, 2:3]),
                        ])
                        S.wait("tensor", "vector")
                        S.wait("tensor", "scalar")

                        def _tp(te):
                            ptv = PT[:, :].bitcast(BF16)
                            for j in range(5):
                                r = te.transpose(ptv[:, j * 128:(j + 1) * 128],
                                                 ebuf[:, j * 128:(j + 1) * 128], ident[:, :])
                            return r
                        S.step("tensor", _tp)
                        S.wait("scalar", "tensor")
                        S.step("scalar", lambda se: se.copy(
                            out=ptb[:, 0:640], in_=PT[:, :].bitcast(BF16)[:, 0:640]))
                        S.wait("tensor", "scalar")

                        def _pv(te, l=l, cw=cw, qt=qt, h=h, prow=prow):
                            for j in range(qt, qt + 5):
                                if j < NQT:
                                    if cw == 0:
                                        v_ = vp[:, l * NQT * INNER + j * INNER + h * 64:
                                                l * NQT * INNER + j * INNER + (h + 1) * 64]
                                    else:
                                        v_ = vtm[:, ((cw - 1) * NQT + j) * 512 + h * 64:
                                                 ((cw - 1) * NQT + j) * 512 + (h + 1) * 64]
                                else:
                                    v_ = vtm[:, (cw * NQT + j - NQT) * 512 + h * 64:
                                             (cw * NQT + j - NQT) * 512 + (h + 1) * 64]
                                r = te.matmul(PS[prow:prow + 64, qt * 128:(qt + 1) * 128], v_,
                                              ptb[:, (j - qt) * 128:(j - qt + 1) * 128],
                                              start=(j == qt), stop=(j == qt + 4),
                                              tile_position=(0, prow))
                            return r
                        S.step("tensor", _pv)
                  S.wait("scalar", "tensor")
                  S.step("scalar", lambda se, ic=ic:
                         se.copy(out=o1b[:, ic * WIN:(ic + 1) * WIN], in_=PS[:, 0:WIN]))

                for oc in range(NG):
                    pb = PA[oc % 2]
                    S.wait("tensor", "scalar")
                    S.wait("tensor", "vector")

                    def _wo2(te, oc=oc, pb=pb):
                        for ic in range(NIC):
                            r = te.matmul(pb[:, 0:WIN],
                                          wuni[:, S2 + ic * C + oc * 128:S2 + ic * C + (oc + 1) * 128],
                                          o1b[:, ic * WIN:(ic + 1) * WIN],
                                          start=(ic == 0), stop=(ic == NIC - 1))
                        return r
                    S.step("tensor", _wo2)
                    S.wait("vector", "tensor")

                    def _xm1(ve, oc=oc, pb=pb, cw=cw, l=l):
                        dst = str_sl(oc, cw * WIN, cw * WIN + WIN)
                        if l == 0:
                            src1 = xn_c[:, oc * (CW + 1) + 1 + cw * WIN:
                                        oc * (CW + 1) + 1 + cw * WIN + WIN]
                            return ve.tensor_add(out=dst, in0=pb[:, 0:WIN], in1=src1)
                        else:
                            return ve.tensor_add(out=dst, in0=pb[:, 0:WIN], in1=dst)
                    S.step("vector", _xm1)

            if whi - 1 >= FA[l]:
                cwl = whi - 1 - wlo
                S.wait("vector", "tensor")
                S.wait("vector", "scalar")
                S.wait("vector", "vector", S.c["vector"])

                def _kvc(ve, cwl=cwl, l=l):
                    for ic in range(NIC):
                        ve.tensor_copy(out=kp[:, (l * NIC + ic) * WIN:(l * NIC + ic + 1) * WIN],
                                       in_=q_sl("k", ic, cwl * WIN, cwl * WIN + WIN))
                    for tc in range(NQT):
                        r = ve.tensor_copy(out=vp[:, l * NQT * INNER + tc * INNER:
                                                  l * NQT * INNER + (tc + 1) * INNER],
                                           in_=vtm[:, (cwl * NQT + tc) * 512:(cwl * NQT + tc + 1) * 512])
                    return r
                S.step("vector", _kvc)

            if (whi - 1 >= FB[l]) and (whi - 1 < FC[l]):
                S.wait("vector", "tensor")
                S.wait("vector", "vector", S.c["vector"])

                def _mc0(ve, cwl=whi - 1 - wlo, l=l):
                    for g in range(NG):
                        r = ve.tensor_copy(out=colc[:, (0 if l == 0 else 2) * NG + g:
                                                (0 if l == 0 else 2) * NG + g + 1],
                                       in_=str_sl(g, cwl * WIN + WIN - 1, cwl * WIN + WIN))
                    return r
                S.step("vector", _mc0)

            for w in wsC:
                cw = w - wlo
                if w == FC[l] and w - 1 >= wlo and w - 1 < FC[l]:
                    S.wait("vector", "tensor")
                    S.wait("vector", "vector", S.c["vector"])

                    def _mc1(ve, cwp=cw - 1, l=l):
                        for g in range(NG):
                            r = ve.tensor_copy(out=colc[:, (0 if l == 0 else 2) * NG + g:
                                                    (0 if l == 0 else 2) * NG + g + 1],
                                           in_=str_sl(g, cwp * WIN + WIN - 1, cwp * WIN + WIN))
                        return r
                    S.step("vector", _mc1)

                S.wait("tensor", "vector")
                ccol = (0 if l == 0 else 2) * NG
                for g in range(NG):
                    S.wait("scalar", "tensor")
                    S.wait("scalar", "vector")

                    def _sqc(se, g=g, cw=cw, ccol=ccol):
                        if g < 4:
                            return se.activation(out=sq2f[:, 0:WIN],
                                                 in_=str_sl(g, cw * WIN, cw * WIN + WIN),
                                                 func=AF.Square)
                        se.activation(out=sq2f[:, 1:WIN],
                                      in_=str_sl(g, cw * WIN, cw * WIN + WIN - 1),
                                      func=AF.Square)
                        return se.activation(out=sq2f[:, 0:1],
                                             in_=colc[:, ccol + g:ccol + g + 1],
                                             func=AF.Square)
                    sc = S.step("scalar", _sqc)
                    S.wait("tensor", "scalar", sc)

                    def _mm2(te, g=g, cw=cw, ccol=ccol):
                        if g < 4:
                            te.matmul(PS[0:1, 0:WIN], ones,
                                      str_sl(g, cw * WIN, cw * WIN + WIN),
                                      start=(g == 0), stop=False)
                        else:
                            te.matmul(PS[0:1, 1:WIN], ones,
                                      str_sl(g, cw * WIN, cw * WIN + WIN - 1),
                                      start=False, stop=False)
                            te.matmul(PS[0:1, 0:1], ones, colc[:, ccol + g:ccol + g + 1],
                                      start=False, stop=(g == NG - 1))
                        r = te.matmul(PB[0:1, 0:WIN], ones, sq2f[:, 0:WIN],
                                      start=(g == 0), stop=(g == NG - 1))
                        return r
                    S.step("tensor", _mm2)

                S.wait("vector", "tensor")

                S.seq("vector", [
                    lambda ve: ve.tensor_scalar(out=strow[0:1, 1024:1024 + WIN],
                                                in0=PS[0:1, 0:WIN], scalar1=1.0 / C,
                                                scalar2=None, op0=ALU.mult),
                    lambda ve: ve.tensor_mul(out=strow[0:1, 0:WIN],
                                             in0=strow[0:1, 1024:1024 + WIN],
                                             in1=strow[0:1, 1024:1024 + WIN]),
                    lambda ve: ve.scalar_tensor_tensor(out=strow[0:1, 0:WIN],
                                                       in0=PB[0:1, 0:WIN], scalar=1.0 / C,
                                                       in1=strow[0:1, 0:WIN],
                                                       op0=ALU.mult, op1=ALU.subtract),
                ])
                S.wait("scalar", "vector")
                S.seq("scalar", [
                    lambda se: se.activation(out=strow[0:1, 0:WIN], in_=strow[0:1, 0:WIN],
                                             func=AF.Ln, bias=cons_f[0:1, 258:259]),
                    lambda se: se.activation(out=strow[0:1, 0:WIN], in_=strow[0:1, 0:WIN],
                                             func=AF.Exp, scale=-0.5),
                ])
                S.wait("vector", "scalar")
                S.step("vector", lambda ve: ve.tensor_mul(
                    out=strow[0:1, WIN:2 * WIN], in0=strow[0:1, 0:WIN],
                    in1=strow[0:1, 1024:1024 + WIN]))
                S.wait("tensor", "vector")

                def _bc2(te):
                    te.matmul(PA[2][:, 0:WIN], bc16[0:1, 0:128],
                              strow[0:1, 0:WIN], start=True, stop=True)
                    return te.matmul(PA[3][:, 0:WIN], bc16[0:1, 0:128],
                                     strow[0:1, WIN:2 * WIN], start=True, stop=True)
                S.step("tensor", _bc2)
                S.wait("vector", "tensor")

                lgo = l * (3 * NG + 2)
                fns = []
                for g in range(NG):
                    lg = sv[:, lgo + g:lgo + g + 1]
                    lb = sv[:, lgo + NG + g:lgo + NG + g + 1]
                    buf2 = t1b[:, (g % 2) * WIN:(g % 2) * WIN + WIN]
                    if g < 4:
                        fns.append(lambda ve, g=g, cw=cw, buf2=buf2: ve.tensor_mul(
                            out=buf2, in0=str_sl(g, cw * WIN, cw * WIN + WIN),
                            in1=PA[2][:, 0:WIN]))
                    else:
                        fns.append(lambda ve, g=g, cw=cw, buf2=buf2, ccol=ccol: (
                            ve.tensor_mul(out=buf2[:, 1:WIN],
                                          in0=str_sl(g, cw * WIN, cw * WIN + WIN - 1),
                                          in1=PA[2][:, 1:WIN]),
                            ve.tensor_mul(out=buf2[:, 0:1],
                                          in0=colc[:, ccol + g:ccol + g + 1],
                                          in1=PA[2][:, 0:1]))[1])
                    fns += [
                        lambda ve, buf2=buf2: ve.tensor_sub(out=buf2, in0=buf2,
                                                            in1=PA[3][:, 0:WIN]),
                        lambda ve, g=g, lg=lg, lb=lb, buf2=buf2: ve.tensor_scalar(
                            out=lno[:, g * WIN:(g + 1) * WIN], in0=buf2,
                            scalar1=lg, scalar2=lb, op0=ALU.mult, op1=ALU.add),
                    ]
                S.seq("vector", fns)

                S.wait("vector", "vector", S.c["vector"])

                def _capw(ve, cw=cw, ccol=ccol):
                    for g in range(NG):
                        r = ve.tensor_copy(out=colc[:, ccol + g:ccol + g + 1],
                                       in_=str_sl(g, cw * WIN + WIN - 1, cw * WIN + WIN))
                    return r
                S.step("vector", _capw)

                for sx in range(6):
                    sbx = sx % 2
                    S.wait("sync", "tensor")
                    for g in range(NG):
                        S.dma("sync",
                              wuni[:, S1 + sbx * 7280 + g * 910:S1 + sbx * 7280 + g * 910 + 455],
                              s_w1[l, g * 128:(g + 1) * 128, sx * SX:(sx + 1) * SX])
                        S.dma("sync",
                              wuni[:, S1 + sbx * 7280 + g * 910 + 455:S1 + sbx * 7280 + (g + 1) * 910],
                              s_w1[l, g * 128:(g + 1) * 128, HID + sx * SX:HID + (sx + 1) * SX])
                    w1ld = S.c["d_sy"]
                    S.drain("sync")
                    S.wait_dma("tensor", "d_sy", w1ld)
                    co = 0
                    for ci, cn in enumerate(SXCH):
                        S.wait("tensor", "vector")
                        S.wait("tensor", "scalar")

                        def _w1m(te, sbx=sbx, co=co, cn=cn):
                            for d in range(NG):
                                te.matmul(PA[0][0:cn, 0:WIN],
                                          wuni[:, S1 + sbx * 7280 + d * 910 + co:
                                               S1 + sbx * 7280 + d * 910 + co + cn],
                                          lno[:, d * WIN:(d + 1) * WIN],
                                          start=(d == 0), stop=(d == NG - 1))
                            for d in range(NG):
                                r = te.matmul(PA[1][0:cn, 0:WIN],
                                              wuni[:, S1 + sbx * 7280 + d * 910 + 455 + co:
                                                   S1 + sbx * 7280 + d * 910 + 455 + co + cn],
                                              lno[:, d * WIN:(d + 1) * WIN],
                                              start=(d == 0), stop=(d == NG - 1))
                            return r
                        S.step("tensor", _w1m)
                        S.wait("scalar", "tensor")
                        S.wait("scalar", "vector")

                        def _gel(se, l=l, sx=sx, ci=ci, cn=cn):
                            bcol = l * 48 + sx * 8
                            se.add(out=sq2f[0:cn, 0:WIN], in_=PA[0][0:cn, 0:WIN],
                                   add=b1s[0:cn, bcol + 2 * ci:bcol + 2 * ci + 1])
                            return se.activation(out=t1b[0:cn, 0:WIN], in_=PA[1][0:cn, 0:WIN],
                                                 func=AF.Gelu,
                                                 bias=b1s[0:cn, bcol + 2 * ci + 1:bcol + 2 * ci + 2])
                        S.step("scalar", _gel)
                        S.wait("vector", "scalar")
                        S.step("vector", lambda ve, sx=sx, ci=ci, cn=cn:
                               ve.tensor_mul(out=agb[0:cn, (sx * 4 + ci) * WIN:(sx * 4 + ci + 1) * WIN],
                                             in0=sq2f[0:cn, 0:WIN], in1=t1b[0:cn, 0:WIN]))
                        co += cn

                for sx in range(6):
                    sbx = sx % 2
                    S.wait("sync", "tensor")
                    co = 0
                    for ci, cn in enumerate(SXCH):
                        S.dma("sync",
                              wuni[0:cn, S2 + sbx * 4096 + ci * C:S2 + sbx * 4096 + (ci + 1) * C],
                              s_w2[l, sx * SX + co:sx * SX + co + cn, :])
                        co += cn
                    w2ld = S.c["d_sy"]
                    S.drain("sync")
                    S.wait_dma("tensor", "d_sy", w2ld)
                    if sx == 0:
                        S.wait("tensor", "vector")

                    def _w2m(te, sx=sx, sbx=sbx):
                        for oc in range(NG):
                            for ci, cn in enumerate(SXCH):
                                r = te.matmul(w2psum[oc],
                                              wuni[0:cn, S2 + sbx * 4096 + ci * C + oc * 128:
                                                   S2 + sbx * 4096 + ci * C + (oc + 1) * 128],
                                              agb[0:cn, (sx * 4 + ci) * WIN:(sx * 4 + ci + 1) * WIN],
                                              start=(sx == 0 and ci == 0),
                                              stop=(sx == 5 and ci == 3))
                        return r
                    S.step("tensor", _w2m)

                S.wait("vector", "tensor")

                def _x1(ve, cw=cw, l=l):
                    for oc in range(NG):
                        b2c = sv[:, l * (3 * NG + 2) + 2 * NG + oc:
                                 l * (3 * NG + 2) + 2 * NG + oc + 1]
                        dst = str_sl(oc, cw * WIN, cw * WIN + WIN)
                        r = ve.scalar_tensor_tensor(out=dst, in0=w2psum[oc], scalar=b2c,
                                                    in1=dst, op0=ALU.add, op1=ALU.add)
                    return r
                S.step("vector", _x1)

                if l == 0 and w == whi - 1:
                    S.wait("vector", "vector", S.c["vector"])

                    def _x1c(ve, cw=cw):
                        for g in range(NG):
                            r = ve.tensor_copy(out=colc[:, 4 * NG + g:4 * NG + g + 1],
                                           in_=str_sl(g, cw * WIN + WIN - 1, cw * WIN + WIN))
                        return r
                    S.step("vector", _x1c)

                if l == 1 and w >= W0:
                    S.wait("gpsimd", "vector")
                    for oc in range(NG):
                        S.dma_out(outT[oc * 128:(oc + 1) * 128,
                                       (w - W0) * WIN:(w - W0 + 1) * WIN],
                                  str_sl(oc, cw * WIN, cw * WIN + WIN))

            S.barrier()

    S.wait_dma("gpsimd", "d_out")

    with es, nc.Block() as block, \
         nc.semaphore("s_te") as s_te, nc.semaphore("s_ve") as s_ve, \
         nc.semaphore("s_se") as s_se, nc.semaphore("s_sy") as s_sy, \
         nc.semaphore("s_ge") as s_ge, nc.semaphore("d_sy") as d_sy, \
         nc.semaphore("d_ge") as d_ge, nc.semaphore("d_out") as d_out, \
         nc.semaphore("d_x0") as d_x0, nc.semaphore("d_x1") as d_x1, \
         nc.semaphore("d_pl") as d_pl, nc.semaphore("d_ps") as d_ps:
        S.sem.update(dict(tensor=s_te, vector=s_ve, scalar=s_se, sync=s_sy,
                          gpsimd=s_ge, d_sy=d_sy, d_ge=d_ge, d_out=d_out,
                          d_x0=d_x0, d_x1=d_x1, d_pl=d_pl, d_ps=d_ps))
        block.tensor(lambda te: [f(te) for f in S.q["tensor"]])
        block.vector(lambda ve: [f(ve) for f in S.q["vector"]])
        block.scalar(lambda se: [f(se) for f in S.q["scalar"]])
        block.sync(lambda sy: [f(sy) for f in S.q["sync"]])
        block.gpsimd(lambda ge: [f(ge) for f in S.q["gpsimd"]])
    return nc


def _host_tables(PRE, OWN, n_total, own_starts):
    TB = PRE + OWN
    half = np.arange(0, DH, 2, dtype=np.float64)
    inv_freq = 1.0 / (10000.0 ** (half / DH))
    base = (half + 0.4 * DH) / (1.4 * DH)
    tabs = []
    for s0 in own_starts:
        pos = np.arange(s0 - PRE, s0 - PRE + TB, dtype=np.float64)
        fr = pos[None, :] * inv_freq[:, None]
        power = (pos - n_total // 2) / (WIN // 2)
        scale = base[:, None] ** power[None, :]
        cos = np.cos(fr)
        sin = np.sin(fr)
        cq = np.concatenate([cos * scale, cos * scale], 0) * 8.0
        sq = np.concatenate([-sin * scale, sin * scale], 0) * 8.0
        ck = np.concatenate([cos / scale, cos / scale], 0)
        sk = np.concatenate([-sin / scale, sin / scale], 0)
        t = np.concatenate([np.tile(cq, (2, 1)).reshape(128, TB),
                            np.tile(sq, (2, 1)).reshape(128, TB),
                            np.tile(ck, (2, 1)).reshape(128, TB),
                            np.tile(sk, (2, 1)).reshape(128, TB)], axis=1)
        tabs.append(np.ascontiguousarray(t.astype(ml_dtypes.bfloat16)))
    return tabs


def _mask_w0(own_start):
    q = own_start + np.arange(WIN)[:, None]
    j = own_start - WIN + np.arange(2 * WIN)[None, :]
    masked = (q < j) | (q > j + WIN) | (j < 0)
    return np.where(masked, np.float32(NEGM), np.float32(0.0)).astype(np.float32)


def kernel(L0_attn_x, L0_attn_time_decay, L0_attn_time_first, L0_attn_time_mix,
           L0_attn_Wk, L0_attn_Wv, L0_attn_Wr, L0_attn_Wo,
           L0_ffn_wqkv, L0_ffn_q_scale, L0_ffn_k_scale, L0_ffn_wo,
           L0_ffn_ln_g, L0_ffn_ln_b, L0_ffn_w1, L0_ffn_b1, L0_ffn_w2, L0_ffn_b2):
    B, T, _ = L0_attn_x.shape
    PRE = T // 2
    OWN = T // 2
    n_cores = 8
    nc = build_nc(PRE, OWN)

    f32 = lambda a: np.ascontiguousarray(np.asarray(a, dtype=np.float32))
    x = f32(L0_attn_x)
    wk_t = f32(np.asarray(L0_attn_Wk).T)
    wv_t = f32(np.asarray(L0_attn_Wv).T)
    wr_t = f32(np.asarray(L0_attn_Wr).T)
    wo_t = f32(np.asarray(L0_attn_Wo).T)
    tpar = np.zeros((128, 3 * NG), np.float32)
    tpar[:, 0:NG] = f32(L0_attn_time_decay).reshape(NG, 128).T
    tpar[:, NG:2 * NG] = f32(L0_attn_time_first).reshape(NG, 128).T
    tpar[:, 2 * NG:3 * NG] = f32(L0_attn_time_mix).reshape(NG, 128).T
    sv = np.zeros((128, 2 * (3 * NG + 2)), np.float32)
    for l in range(2):
        b = l * (3 * NG + 2)
        sv[:, b:b + NG] = f32(L0_ffn_ln_g[l]).reshape(NG, 128).T
        sv[:, b + NG:b + 2 * NG] = f32(L0_ffn_ln_b[l]).reshape(NG, 128).T
        sv[:, b + 2 * NG:b + 3 * NG] = f32(L0_ffn_b2[l]).reshape(NG, 128).T
        sv[:, b + 3 * NG:b + 3 * NG + 1] = np.tile(f32(L0_ffn_q_scale[l]), 2)[:, None]
        sv[:, b + 3 * NG + 1:b + 3 * NG + 2] = np.tile(f32(L0_ffn_k_scale[l]), 2)[:, None]
    b1v = f32(L0_ffn_b1)
    b1p = np.zeros((128, 2 * 6 * 8), np.float32)
    for l in range(2):
        for sx in range(6):
            co = 0
            for ci, cn in enumerate(SXCH):
                b1p[0:cn, l * 48 + sx * 8 + 2 * ci] = b1v[l, sx * SX + co:sx * SX + co + cn]
                b1p[0:cn, l * 48 + sx * 8 + 2 * ci + 1] = b1v[l, HID + sx * SX + co:HID + sx * SX + co + cn]
                co += cn
    ident = np.eye(128, dtype=ml_dtypes.bfloat16)
    perm = np.zeros((128, 128), dtype=ml_dtypes.bfloat16)
    for i in range(128):
        j = i + 32 if (i % 64) < 32 else i - 32
        perm[j, i] = 1.0
    bc16 = np.kron(np.eye(16, dtype=np.float32), np.ones((1, 128), np.float32))
    hind = np.zeros((128, NIC * H), np.float32)
    for ic in range(NIC):
        for r in range(128):
            hind[r, ic * H + ic * 2 + r // 64] = 1.0
    hindT = np.zeros((H, INNER), np.float32)
    for pcol in range(INNER):
        hindT[pcol // 64, pcol] = 1.0
    masks = np.zeros((128, 260), np.float32)
    rr, cc2 = np.meshgrid(np.arange(128), np.arange(128), indexing="ij")
    masks[:, 0:128] = np.where(rr <= cc2, 0.0, NEGM)
    masks[:, 128:256] = np.where(rr >= cc2, 0.0, NEGM)
    masks[:, 256] = 1.0
    masks[:, 258] = 1e-5
    masks[:, 259] = 1e-24

    tabs = _host_tables(PRE, OWN, T, [0 if i % 2 == 0 else PRE for i in range(n_cores)])

    shared = dict(wkin=wk_t, wvin=wv_t, wrin=wr_t, woin=wo_t, tparams=tpar,
                  wqkv=f32(L0_ffn_wqkv), wof=f32(L0_ffn_wo), w1=f32(L0_ffn_w1),
                  w2=f32(L0_ffn_w2), svecs=sv, b1p=b1p, ident_in=ident, perm_in=perm,
                  bc16in=bc16, hindin=hind, hindTin=hindT)

    in_maps = []
    for i in range(n_cores):
        b = i // 2
        odd = i % 2 == 1
        xb = np.ascontiguousarray(x[b].T)
        if odd:
            xcore = xb
        else:
            xcore = np.concatenate([np.zeros((C, PRE), np.float32), xb[:, :OWN]], axis=1)
        mk = masks.copy()
        mk[:, 257] = 1.0 if odd else 0.0
        m = dict(shared)
        m.update(xT=np.ascontiguousarray(xcore),
                 masks_in=mk,
                 mask_w0=_mask_w0(PRE if odd else 0),
                 rott_in=tabs[i])
        in_maps.append(m)

    res = run_bass_kernel_spmd(nc, in_maps, core_ids=list(range(n_cores)))
    out = np.empty((B, T, C), np.float32)
    for i in range(n_cores):
        b = i // 2
        sl = slice(PRE, T) if i % 2 == 1 else slice(0, OWN)
        out[b, sl, :] = res.results[i]["outT"].T
    return out
